# version 23
# baseline (speedup 1.0000x reference)
import math
from contextlib import ExitStack

import numpy as np

import concourse.bass as bass
import concourse.mybir as mybir
from concourse.bass_utils import run_bass_kernel_spmd

F32 = mybir.dt.float32
BF16 = mybir.dt.bfloat16
AF = mybir.ActivationFunctionType
ALU = mybir.AluOpType
AX = mybir.AxisListType

D = 2048
H = 16
HD = 128
INTER = 5632
SEQ = 2048
NSEQ = 2
T = NSEQ * SEQ
NTT = T // 128
NTG = T // 512
NIC = INTER // 128
NIG2 = INTER // 256
RS = 1.0 / math.sqrt(HD)
EPS = 1e-5
NEGINF = -1e9


class Ctr:

    def __init__(self, nc, name):
        self.sem = nc.alloc_semaphore(name)
        self.n = 0

    def inc(self, inst, k=1):
        inst.then_inc(self.sem, k)
        self.n += k
        return self.n

    def wait(self, eng, n=None):
        eng.wait_ge(self.sem, self.n if n is None else n)


class B:
    def __init__(self, dbg=()):
        self.nc = bass.Bass("TRN2", target_bir_lowering=False)
        self.dbg = set(dbg)
        self.ctrs = {}
        self.stack = ExitStack()
        self.bar_ctr = Ctr(self.nc, "bar")
        self.bar_round = {}
        nc = self.nc

        self.dbg_pairs = []

        def dram(name, shape, dtype, kind="Internal"):
            ap = nc.dram_tensor(name, shape, dtype, kind=kind).ap()
            if name in self.dbg:
                dbg_ap = nc.dram_tensor(name + "__dbg", shape, F32,
                                        kind="ExternalOutput").ap()
                self.dbg_pairs.append((ap, dbg_ap))
            return ap

        self.x = nc.dram_tensor("x", [NSEQ, SEQ, D], F32, kind="ExternalInput").ap()
        self.w_qkv = nc.dram_tensor("w_qkv", [3 * D, D], F32, kind="ExternalInput").ap()
        self.b_qkv = nc.dram_tensor("b_qkv", [3 * D], F32, kind="ExternalInput").ap()
        self.w_out = nc.dram_tensor("w_out", [D, D], F32, kind="ExternalInput").ap()
        self.b_out = nc.dram_tensor("b_out", [D], F32, kind="ExternalInput").ap()
        self.gate_w = nc.dram_tensor("gate_w", [INTER, D], F32, kind="ExternalInput").ap()
        self.up_w = nc.dram_tensor("up_w", [INTER, D], F32, kind="ExternalInput").ap()
        self.down_w = nc.dram_tensor("down_w", [D, INTER], F32, kind="ExternalInput").ap()
        self.gate_s = nc.dram_tensor("gate_s", [INTER], F32, kind="ExternalInput").ap()
        self.up_s = nc.dram_tensor("up_s", [INTER], F32, kind="ExternalInput").ap()
        self.down_s = nc.dram_tensor("down_s", [D], F32, kind="ExternalInput").ap()
        self.y = nc.dram_tensor("y", [NSEQ, SEQ, D], F32, kind="ExternalOutput").ap()

        self.wq16 = dram("wq16", [D, D], BF16)
        self.wk16 = dram("wk16", [D, D], BF16)
        self.wv16 = dram("wv16", [D, D], BF16)
        self.g16 = dram("g16", [INTER, D], BF16)
        self.u16 = dram("u16", [INTER, D], BF16)
        self.dw16 = dram("dw16", [D, INTER], BF16)
        self.g16T = dram("g16T", [D, INTER], BF16)
        self.u16T = dram("u16T", [D, INTER], BF16)
        self.dw16T = dram("dw16T", [INTER, D], BF16)
        self.wo16 = dram("wo16", [D, D], BF16)

        self.xn_scr = dram("xn_scr", [NTG, 128, 16, 512], BF16)
        self.q_scr = dram("q_scr", [H, 128, T], BF16)
        self.k_scr = dram("k_scr", [H, 128, T], BF16)
        self.v_scr = dram("v_scr", [NSEQ, H, 16, 128, 128], BF16)
        self.hdn_scr = dram("hdn_scr", [NIC, 128, T], BF16)
        self.ao_scr = dram("ao_scr", [H, 128, T], BF16)
        self.ffn_scr = dram("ffn_scr", [NTT, 128, D], BF16)

    def ctr(self, name):
        if name not in self.ctrs:
            self.ctrs[name] = Ctr(self.nc, name)
        return self.ctrs[name]

    def sb(self, stk, name, shape, dtype):
        return stk.enter_context(self.nc.sbuf_tensor(name, shape, dtype))

    def psum(self, stk, name, shape, dtype=F32):
        return stk.enter_context(self.nc.psum_tensor(name, shape, dtype))

    def vs(self, eng, inst):
        c = self.ctr("self_" + str(eng.engine))
        c.inc(inst)
        c.wait(eng)

    def _wtr(self, sy, k, wstg, s_gt, s_gts, HI):
        if k < 64:
            if k >= 2:
                s_gts.wait(sy, (k - 1) * 16)
            w16 = self.g16 if k < 32 else self.u16
            dc, piece = (k % 32) // 2, k % 2
            s_gt.inc(sy.dma_start_transpose(
                out=wstg[k % 2][:],
                in_=w16[piece * HI:(piece + 1) * HI,
                        dc * 128:(dc + 1) * 128]), 16)
        if k >= 1:
            ks = k - 1
            w16T = self.g16T if ks < 32 else self.u16T
            dc, piece = (ks % 32) // 2, ks % 2
            s_gt.wait(sy, (ks + 1) * 16)
            s_gts.inc(sy.dma_start(
                out=w16T[dc * 128:(dc + 1) * 128,
                         piece * HI:(piece + 1) * HI],
                in_=wstg[ks % 2][:]), 16)

    def _wtr_end(self):
        pass

    def wbuild(self, tag, src_rows, wT_all, fstg, bstg, wtp, base=0):
        ld = self.ctr(tag + "_ld")
        ca = self.ctr(tag + "_ca")
        tr = self.ctr(tag + "_tr")
        ev = self.ctr(tag + "_ev")

        def sy_emit(sy):
            for c in range(32):
                ec, half = c // 2, c % 2
                if base + c >= 2:
                    ca.wait(sy, base + c - 1)
                ld.inc(sy.dma_start(
                    out=fstg[c % 2][:],
                    in_=src_rows[ec * 128:(ec + 1) * 128,
                                 half * 1024:(half + 1) * 1024]), 16)

        def ac_emit(a):
            for c in range(32):
                ld.wait(a, (base + c + 1) * 16)
                if base + c >= 2:
                    tr.wait(a, base + c - 1)
                ca.inc(a.activation(bstg[c % 2][:], fstg[c % 2][:], AF.Identity,
                                    bias=0.0, scale=1.0))

        def pe_emit(t):
            for c in range(32):
                ec, half = c // 2, c % 2
                ca.wait(t, base + c + 1)
                if base + c >= 2:
                    ev.wait(t, base + c - 1)
                hoff = (c % 2) * 1024
                for d2 in range(8):
                    ins = t.transpose(wtp[:, hoff + d2 * 128:hoff + (d2 + 1) * 128],
                                      bstg[c % 2][:, d2 * 128:(d2 + 1) * 128],
                                      self.ident_bf[:])
                tr.inc(ins)

        def ve_emit(v):
            for c in range(32):
                ec, half = c // 2, c % 2
                tr.wait(v, base + c + 1)
                hoff = (c % 2) * 1024
                dst = wT_all[:].rearrange("p (dc e) -> p dc e", e=2048)[
                    :, half * 8:half * 8 + 8, ec * 128:(ec + 1) * 128]
                ev.inc(v.tensor_copy(
                    dst, wtp[:, hoff:hoff + 1024].rearrange("p (a b) -> p a b", b=128)))

        return sy_emit, ac_emit, pe_emit, ve_emit, ev

    def bar(self, eng, who):
        r = self.bar_round.get(who, 0) + 1
        self.bar_round[who] = r
        eng.drain()
        eng.nop().then_inc(self.bar_ctr.sem, 1)
        eng.wait_ge(self.bar_ctr.sem, 5 * r)

    def build(self):
        nc = self.nc
        cst = self.stack
        self.ident_bf = self.sb(cst, "ident_bf", [128, 128], BF16)
        self.ident_f = self.sb(cst, "ident_f", [128, 128], F32)
        self.maskU = self.sb(cst, "maskU", [128, 128], BF16)
        self.ones_col = self.sb(cst, "ones_col", [1, 128], F32)
        self.bq_cols = self.sb(cst, "bq_cols", [128, 32], F32)
        self.bqs_cols = self.sb(cst, "bqs_cols", [128, 16], F32)
        self.gs_cols = self.sb(cst, "gs_cols", [128, NIC], F32)
        self.us_cols = self.sb(cst, "us_cols", [128, NIC], F32)

        with nc.Block() as block:
            self.blk = block
            self.ph_consts()
            self.ph_p1()
            self.ph_p2()
            self.ph_p34()
            self.ph_p5()
            self.ph_p6()
            if self.dbg_pairs:
                self.emit_barrier()
                dbgc = self.ctr("dbgc")

                def gp_dbg(g):
                    for ap, dbg_ap in self.dbg_pairs:
                        dbgc.inc(g.dma_start(out=dbg_ap, in_=ap), 16)
                    dbgc.wait(g)
                self.blk.gpsimd(gp_dbg)
        self.stack.close()
        return nc

    def ph_consts(self):
        stk = ExitStack()
        bqr = self.sb(stk, "c_bqr", [128, 128], F32)
        gsr = self.sb(stk, "c_gsr", [128, 128], F32)
        usr = self.sb(stk, "c_usr", [128, 128], F32)
        tp = self.psum(stk, "c_tp", [128, 128], F32)

        c_ld = self.ctr("c_ld")
        c_g = self.ctr("c_g")
        c_pe = self.ctr("c_pe")
        c_v = self.ctr("c_v")
        pc = {w: self.ctr("pc_" + w) for w in ("wq", "wk", "wv", "g", "u", "dw", "wo")}
        self.pc = pc

        def gp(g):
            self.vs(g, g.memset(self.ident_bf[:], 0.0))
            g.affine_select(out=self.ident_bf[:], in_=self.ident_bf[:],
                            compare_op=ALU.not_equal, fill=1.0, base=0,
                            pattern=[[-1, 128]], channel_multiplier=1)
            self.vs(g, g.memset(self.ident_f[:], 0.0))
            g.affine_select(out=self.ident_f[:], in_=self.ident_f[:],
                            compare_op=ALU.not_equal, fill=1.0, base=0,
                            pattern=[[-1, 128]], channel_multiplier=1)
            self.vs(g, g.memset(self.maskU[:], 0.0))
            g.affine_select(out=self.maskU[:], in_=self.maskU[:],
                            compare_op=ALU.is_ge, fill=NEGINF, base=0,
                            pattern=[[-1, 128]], channel_multiplier=1)
            c_g.inc(g.memset(self.ones_col[:], 1.0))
            g.memset(bqr[:], 0.0)
            g.memset(gsr[:], 0.0)
            c_g.inc(g.memset(usr[:], 0.0))
            casts = [
                (self.gate_w, self.g16, 0, INTER, D, pc["g"], 512),
                (self.up_w, self.u16, 0, INTER, D, pc["u"], 512),
            ]
            for src, dst, roff, nrows, width, ctr, step in casts:
                for r0 in range(0, nrows, step):
                    r1 = min(r0 + step, nrows)
                    ctr.inc(g.dma_start(out=dst[r0:r1, :],
                                        in_=src[roff + r0:roff + r1, :]), 16)

        def sy(sy):
            c_g.wait(sy, 2)
            c_ld.inc(sy.dma_start(out=bqr[0:48, :],
                                  in_=self.b_qkv.rearrange("(c p) -> c p", p=128)), 16)
            c_ld.inc(sy.dma_start(out=gsr[0:44, :],
                                  in_=self.gate_s.rearrange("(c p) -> c p", p=128)), 16)
            c_ld.inc(sy.dma_start(out=usr[0:44, :],
                                  in_=self.up_s.rearrange("(c p) -> c p", p=128)), 16)

        def pe(t):
            c_ld.wait(t)
            c_g.wait(t)
            t2 = t.transpose(tp[:, :], bqr[:, :], self.ident_f[:])
            c_pe.inc(t2)
            c_v.wait(t, 1)
            c_pe.inc(t.transpose(tp[:, :], gsr[:, :], self.ident_f[:]))
            c_v.wait(t, 2)
            c_pe.inc(t.transpose(tp[:, :], usr[:, :], self.ident_f[:]))

        def ve(v):
            c_pe.wait(v, 1)
            ins = v.tensor_copy(self.bq_cols[:, :], tp[:, 0:32])
            c_v.inc(ins)
            c_v.wait(v)
            v.tensor_scalar_mul(self.bqs_cols[:, :], self.bq_cols[:, 0:16], RS)
            c_pe.wait(v, 2)
            c_v.inc(v.tensor_copy(self.gs_cols[:, :], tp[:, 0:44]))
            c_pe.wait(v, 3)
            c_v.inc(v.tensor_copy(self.us_cols[:, :], tp[:, 0:44]))

        def ac(a):
            pass

        self.blk.gpsimd(gp)
        self.blk.sync(sy)
        self.blk.tensor(pe)
        self.blk.vector(ve)
        self.blk.scalar(ac)
        self.emit_barrier()
        stk.close()

    def emit_barrier(self):
        self.blk.gpsimd(lambda g: self.bar(g, "g"))
        self.blk.sync(lambda s: self.bar(s, "s"))
        self.blk.tensor(lambda t: self.bar(t, "t"))
        self.blk.vector(lambda v: self.bar(v, "v"))
        self.blk.scalar(lambda a: self.bar(a, "a"))

    def ph_p1(self):
        stk = ExitStack()
        wqT_all = self.sb(stk, "p1_wqT", [128, 16 * D], BF16)
        fstg = [self.sb(stk, f"p1_wf{i}", [128, 1024], F32) for i in range(2)]
        bstg = [self.sb(stk, f"p1_wb{i}", [128, 1024], BF16) for i in range(2)]
        xb = [self.sb(stk, f"p1_x{i}", [128, D], F32) for i in range(2)]
        xnb = [self.sb(stk, f"p1_xn{i}", [128, D], BF16) for i in range(4)]
        sq = self.sb(stk, "p1_sq", [128, D], F32)
        st = [self.sb(stk, f"p1_st{i}", [128, 8], F32) for i in range(2)]
        xng = [self.sb(stk, f"p1_xng{i}", [128, 16 * 512], BF16) for i in range(2)]
        qev = [self.sb(stk, f"p1_qe{i}", [128, 512], BF16) for i in range(2)]
        trp = [self.psum(stk, f"p1_tr{i}", [128, D], BF16) for i in range(2)]
        qps = [self.psum(stk, f"p1_qp{i}", [128, 512], F32) for i in range(3)]

        s_x = self.ctr("p1_x")
        s_va = self.ctr("p1_va")
        s_ac = self.ctr("p1_ac")
        s_vb = self.ctr("p1_vb")
        s_ad = self.ctr("p1_ad")
        s_vc = self.ctr("p1_vc")
        s_ae = self.ctr("p1_ae")
        s_tr = self.ctr("p1_tr")
        s_ev = self.ctr("p1_ev")
        s_qm = self.ctr("p1_qm")
        s_qe = self.ctr("p1_qe")
        s_stx = self.ctr("p1_stx")
        s_stq = self.ctr("p1_stq")
        wb_sy, wb_ac, wb_pe, wb_ve, w1_ev = self.wbuild(
            "w1", self.w_qkv[0:D], wqT_all, fstg, bstg, trp[0][:])

        def sy(sy):
            for i in range(2):
                s_x.inc(sy.dma_start(out=xb[i % 2][:], in_=self.x[0, i * 128:(i + 1) * 128, :]), 16)
            wb_sy(sy)
            for i in range(2, NTT):
                s, r = i // 16, (i % 16) * 128
                s_ae.wait(sy, i - 1)
                s_x.inc(sy.dma_start(out=xb[i % 2][:], in_=self.x[s, r:r + 128, :]), 16)

        def ve(v):
            wb_ve(v)
            for g in range(NTG):
                for j in range(4):
                    i = g * 4 + j
                    c = st[i % 2]
                    s_x.wait(v, (i + 1) * 16)
                    if i >= 2:
                        s_ae.wait(v, i - 1)
                    self.vs(v, v.reduce_sum(c[:, 0:1], xb[i % 2][:], axis=AX.X))
                    s_va.inc(v.tensor_scalar_mul(c[:, 1:2], c[:, 0:1], -1.0 / D))
                    s_ac.wait(v, i + 1)
                    s_vb.inc(v.tensor_scalar(c[:, 3:4], c[:, 2:3], 1.0 / D, EPS,
                                             ALU.mult, ALU.add))
                    s_ad.wait(v, i + 1)
                    self.vs(v, v.reciprocal(c[:, 5:6], c[:, 4:5]))
                    s_vc.inc(v.tensor_tensor(c[:, 6:7], c[:, 1:2], c[:, 5:6], ALU.mult))
                if g >= 1:
                    qe_group(v, g - 1)
                for j in range(4):
                    i = g * 4 + j
                    off = j * 128
                    if j == 0 and g >= 2:
                        s_qm.wait(v, (g - 1) * 16)
                        s_stx.wait(v, (g - 1) * 16)
                    s_tr.wait(v, i + 1)
                    dst = xng[g % 2][:].rearrange("p (dc t) -> p dc t", dc=16)[:, :, off:off + 128]
                    s_ev.inc(v.tensor_copy(
                        dst, trp[i % 2][:].rearrange("p (dc t) -> p dc t", dc=16)))
            qe_group(v, NTG - 1)

        def qe_group(v, g):
            for ec in range(16):
                k = g * 16 + ec
                s_qm.wait(v, k + 1)
                if k >= 2:
                    s_stq.wait(v, (k - 1) * 16)
                s_qe.inc(v.tensor_scalar(qev[k % 2][:], qps[k % 3][:],
                                         self.bq_cols[:, ec:ec + 1], RS,
                                         ALU.add, ALU.mult))

        def ac(a):
            wb_ac(a)
            for g in range(NTG):
                for j in range(4):
                    i = g * 4 + j
                    c = st[i % 2]
                    s_va.wait(a, i + 1)
                    s_ac.inc(a.activation(sq[:], xb[i % 2][:], AF.Square,
                                          bias=c[:, 1:2], scale=1.0, accum_out=c[:, 2:3]))
                    s_vb.wait(a, i + 1)
                    s_ad.inc(a.activation(c[:, 4:5], c[:, 3:4], AF.Sqrt))
                    s_vc.wait(a, i + 1)
                    if i >= 4:
                        s_tr.wait(a, i - 3)
                    s_ae.inc(a.activation(xnb[i % 4][:], xb[i % 2][:], AF.Identity,
                                          bias=c[:, 6:7], scale=c[:, 5:6]))

        def pe(t):
            wb_pe(t)
            w1_ev.wait(t, 32)
            for g in range(NTG):
                for j in range(4):
                    i = g * 4 + j
                    s_ae.wait(t, i + 1)
                    if i >= 2:
                        s_ev.wait(t, i - 1)
                    for dc in range(16):
                        ins = t.transpose(trp[i % 2][:, dc * 128:(dc + 1) * 128],
                                          xnb[i % 4][:, dc * 128:(dc + 1) * 128],
                                          self.ident_bf[:])
                    s_tr.inc(ins)
                s_ev.wait(t, (g + 1) * 4)
                for ec in range(16):
                    k = g * 16 + ec
                    if k >= 3:
                        s_qe.wait(t, k - 2)
                    for dc in range(16):
                        ins = t.matmul(
                            qps[k % 3][:],
                            wqT_all[:, dc * 2048 + ec * 128:dc * 2048 + (ec + 1) * 128],
                            xng[g % 2][:, dc * 512:(dc + 1) * 512],
                            start=(dc == 0), stop=(dc == 15))
                    s_qm.inc(ins)

        def gp(g):
            for gi in range(NTG):
                s_ev.wait(g, (gi + 1) * 4)
                s_stx.inc(g.dma_start(
                    out=self.xn_scr[gi].rearrange("p dc t -> p (dc t)"),
                    in_=xng[gi % 2][:]), 16)
                for ec in range(16):
                    k = gi * 16 + ec
                    s_qe.wait(g, k + 1)
                    s_stq.inc(g.dma_start(
                        out=self.q_scr[ec][:, gi * 512:(gi + 1) * 512],
                        in_=qev[k % 2][:]), 16)
            s_stx.wait(g)
            s_stq.wait(g)

        self.blk.sync(sy)
        self.blk.vector(ve)
        self.blk.scalar(ac)
        self.blk.tensor(pe)
        self.blk.gpsimd(gp)
        self.emit_barrier()
        stk.close()

    def ph_p2(self):
        stk = ExitStack()
        wkT_all = self.sb(stk, "p2_wkT", [128, 16 * D], BF16)
        wvT_all = self.sb(stk, "p2_wvT", [128, 16 * D], BF16)
        fstg = [self.sb(stk, f"p2_wf{i}", [128, 1024], F32) for i in range(2)]
        bstg = [self.sb(stk, f"p2_wb{i}", [128, 1024], BF16) for i in range(2)]
        xng = [self.sb(stk, f"p2_xg{i}", [128, 16 * 512], BF16) for i in range(2)]
        kev = [self.sb(stk, f"p2_ke{i}", [128, 512], BF16) for i in range(2)]
        vev = [self.sb(stk, f"p2_ve{i}", [128, 512], BF16) for i in range(2)]
        bvrow = self.sb(stk, "p2_bvr", [1, D], F32)
        bv_bc = self.sb(stk, "p2_bvbc", [128, D], F32)
        wstg = [self.sb(stk, f"p2_ws{i}", [128, INTER // 2], BF16) for i in range(2)]
        kps = [self.psum(stk, f"p2_kp{i}", [128, 512], F32) for i in range(3)]
        vps = [self.psum(stk, f"p2_vp{i}", [128, 512], F32) for i in range(2)]
        bcp = self.psum(stk, "p2_bcp", [128, 512], F32)
        wtp = self.psum(stk, "p2_wtp", [128, 2048], BF16)

        s_w = self.ctr("p2_w")
        s_br = self.ctr("p2_br")
        s_bm = self.ctr("p2_bm")
        s_bv = self.ctr("p2_bv")
        s_xg = self.ctr("p2_xg")
        s_km = self.ctr("p2_km")
        s_ke = self.ctr("p2_ke")
        s_vm = self.ctr("p2_vm")
        s_ve = self.ctr("p2_ve")
        s_stk = self.ctr("p2_stk")
        s_stv = self.ctr("p2_stv")
        s_xf = self.ctr("p2_xf")
        s_gt = self.ctr("p2_gt")
        s_gts = self.ctr("p2_gts")

        wk_sy, wk_ac, wk_pe, wk_ve, w2_ev = self.wbuild(
            "w2", self.w_qkv[D:2 * D], wkT_all, fstg, bstg, wtp[:])
        wv_sy, wv_ac, wv_pe, wv_ve, w2_ev = self.wbuild(
            "w2", self.w_qkv[2 * D:3 * D], wvT_all, fstg, bstg, wtp[:], base=32)

        def sy(sy):
            s_br.inc(sy.dma_start(out=bvrow[:], in_=self.b_qkv[None, 2 * D:3 * D]), 16)
            wk_sy(sy)
            wv_sy(sy)
            HI = INTER // 2
            for g in range(NTG):
                if g >= 2:
                    s_km.wait(sy, (g - 1) * 16)
                    s_vm.wait(sy, (g - 1) * 16)
                s_xg.inc(sy.dma_start(
                    out=xng[g % 2][:],
                    in_=self.xn_scr[g].rearrange("p dc t -> p (dc t)")), 16)
                if g >= 2:
                    if g == 2:
                        self.pc["g"].wait(sy)
                    if g == 6:
                        self.pc["u"].wait(sy)
                    for k in range((g - 2) * 8, (g - 1) * 8):
                        self._wtr(sy, k, wstg, s_gt, s_gts, HI)
            for k in range(48, 65):
                self._wtr(sy, k, wstg, s_gt, s_gts, HI)

        def pe(t):
            wk_pe(t)
            wv_pe(t)
            s_br.wait(t)
            for b in range(4):
                if b >= 1:
                    s_bv.wait(t, b)
                s_bm.inc(t.matmul(bcp[:], self.ones_col[:],
                                  bvrow[0:1, b * 512:(b + 1) * 512],
                                  start=True, stop=True))
            w2_ev.wait(t, 64)
            for g in range(NTG):
                s_xg.wait(t, (g + 1) * 16)
                for ec in range(16):
                    k = g * 16 + ec
                    if k >= 3:
                        s_ke.wait(t, k - 2)
                    for dc in range(16):
                        ins = t.matmul(
                            kps[k % 3][:],
                            wkT_all[:, dc * 2048 + ec * 128:dc * 2048 + (ec + 1) * 128],
                            xng[g % 2][:, dc * 512:(dc + 1) * 512],
                            start=(dc == 0), stop=(dc == 15))
                    s_km.inc(ins)
                for ts in range(4):
                    for eb in range(4):
                        k = g * 16 + ts * 4 + eb
                        if k >= 2:
                            s_ve.wait(t, k - 1)
                        for dc in range(16):
                            ins = t.matmul(
                                vps[k % 2][:],
                                xng[g % 2][:, dc * 512 + ts * 128:dc * 512 + (ts + 1) * 128],
                                wvT_all[:, dc * 2048 + eb * 512:dc * 2048 + (eb + 1) * 512],
                                start=(dc == 0), stop=(dc == 15))
                        s_vm.inc(ins)

        def ac(a):
            wk_ac(a)
            wv_ac(a)
            for g in range(NTG):
                for ec in range(16):
                    k = g * 16 + ec
                    s_km.wait(a, k + 1)
                    if k >= 2:
                        s_stk.wait(a, (k - 1) * 16)
                    s_ke.inc(a.activation(kev[k % 2][:], kps[k % 3][:], AF.Identity,
                                          bias=self.bq_cols[:, 16 + ec:17 + ec], scale=1.0))

        def ve(v):
            wk_ve(v)
            wv_ve(v)
            for b in range(4):
                s_bm.wait(v, b + 1)
                s_bv.inc(v.tensor_copy(bv_bc[:, b * 512:(b + 1) * 512], bcp[:]))
            for g in range(NTG):
                for ts in range(4):
                    for eb in range(4):
                        k = g * 16 + ts * 4 + eb
                        s_vm.wait(v, k + 1)
                        if k >= 2:
                            s_stv.wait(v, (k - 1) * 64)
                        s_ve.inc(v.scalar_tensor_tensor(
                            vev[k % 2][:], vps[k % 2][:], 1.0,
                            bv_bc[:, eb * 512:(eb + 1) * 512], ALU.bypass, ALU.add))

        def gp(g):
            for gi in range(NTG):
                s = gi // 4
                for ec in range(16):
                    k = gi * 16 + ec
                    s_ke.wait(g, k + 1)
                    s_stk.inc(g.dma_start(
                        out=self.k_scr[ec][:, gi * 512:(gi + 1) * 512],
                        in_=kev[k % 2][:]), 16)
                for ts in range(4):
                    tt_in_seq = (gi % 4) * 4 + ts
                    for eb in range(4):
                        k = gi * 16 + ts * 4 + eb
                        s_ve.wait(g, k + 1)
                        for hh in range(4):
                            h = eb * 4 + hh
                            s_stv.inc(g.dma_start(
                                out=self.v_scr[s, h, tt_in_seq],
                                in_=vev[k % 2][:, hh * 128:(hh + 1) * 128]), 16)
            s_stk.wait(g)
            s_stv.wait(g)
            s_gts.wait(g)

        self.blk.sync(sy)
        self.blk.tensor(pe)
        self.blk.scalar(ac)
        self.blk.vector(ve)
        self.blk.gpsimd(gp)
        self.emit_barrier()
        stk.close()

    @staticmethod
    def p4_batches(qg):
        out = []
        for pqs in range(4):
            n = qg * 4 + pqs + 1
            kt0 = 0
            while n > 0:
                nb = min(8, n)
                out.append((pqs, kt0, nb))
                kt0 += nb
                n -= nb
        return out

    def ph_p34(self):
        stk = ExitStack()
        kh = [self.sb(stk, f"p4_k{i}", [128, SEQ], BF16) for i in range(2)]
        vh = [self.sb(stk, f"p4_v{i}", [128, SEQ], BF16) for i in range(2)]
        qh = [self.sb(stk, f"p4_q{i}", [128, 512], BF16) for i in range(2)]
        pt = self.sb(stk, "p4_pt", [128, 16 * 512], BF16)
        pb = [self.sb(stk, f"p4_pb{i}", [128, SEQ], BF16) for i in range(2)]
        cols = self.sb(stk, "p4_cols", [128, 16], F32)
        rrow = self.sb(stk, "p4_rrow", [1, 512], F32)
        rbc = self.sb(stk, "p4_rbc", [128, 512], F32)
        aoe = [self.sb(stk, f"p4_ao{i}", [128, 512], BF16) for i in range(2)]
        xnr = [self.sb(stk, f"p3_xn{dc}", [128, T], BF16) for dc in range(16)]
        gT = [self.sb(stk, f"p3_g{i}", [128, 2048], BF16) for i in range(2)]
        uT = [self.sb(stk, f"p3_u{i}", [128, 2048], BF16) for i in range(2)]
        sil = [self.sb(stk, f"p3_sil{i}", [128, 512], F32) for i in range(2)]
        hbf = [self.sb(stk, f"p3_h{i}", [128, 512], BF16) for i in range(2)]
        dstg = [self.sb(stk, "p34_ds0", [128, 2048], BF16)] * 2
        sps = self.psum(stk, "p4_sps", [128, SEQ], F32)
        tps = self.psum(stk, "p4_tps", [128, 2048], BF16)
        aor = self.psum(stk, "p4_aor", [128, 512], F32)
        gup = self.psum(stk, "p3_gup", [128, 512], F32)
        gps = gup
        ups = gup

        s_kv = self.ctr("p4_kv")
        s_q = self.ctr("p4_q")
        s_sc = self.ctr("p4_sc")
        s_mx = self.ctr("p4_mx")
        s_ex = self.ctr("p4_ex")
        s_rc = self.ctr("p4_rc")
        s_tp = self.ctr("p4_tp")
        s_te = self.ctr("p4_te")
        s_rr = self.ctr("p4_rr")
        s_re = self.ctr("p4_re")
        s_rb = self.ctr("p4_rb")
        s_rbe = self.ctr("p4_rbe")
        s_pv = self.ctr("p4_pv")
        s_ae = self.ctr("p4_ae")
        s_sta = self.ctr("p4_sta")
        s_xnr = self.ctr("p3_xnr")
        s_w3 = self.ctr("p3_w")
        s_gm = self.ctr("p3_gm")
        s_um = self.ctr("p3_um")
        s_sil = self.ctr("p3_sil")
        s_hd = self.ctr("p3_hd")
        s_st3 = self.ctr("p3_st")
        s_dt = self.ctr("p34_dt")
        s_dts = self.ctr("p34_dts")

        SH = [(s, h) for s in range(NSEQ) for h in range(H)]

        att_ev = []
        for j in range(32):
            for qg in range(4):
                jq = j * 4 + qg
                batches = self.p4_batches(qg)
                ntr = {p: sum(nb for q_, _, nb in batches if q_ == p) for p in range(4)}
                if jq == 0:
                    att_ev.append((("sc", 0, 0), 800))
                for qs in range(1, 4):
                    c = qg * 512 + (qs + 1) * 128 + 134 + ntr[qs - 1] * 140
                    att_ev.append((("sc", jq, qs), c))
                att_ev.append((("t1", jq), ntr[3] * 140 + 100))
                att_ev.append((("t2", jq), 700))
                if jq + 1 < 128:
                    att_ev.append((("sc", jq + 1, 0), (jq % 4 == 3) * 0 + ((jq + 1) % 4) * 512 + 262))
                att_ev.append((("t3", jq), 2100))
                att_ev.append((("t4", jq), (4 * qg + 4) * 512 + 200))
        p3_ev = []
        for ic in range(NIC):
            for tg in range(NTG):
                p3_ev.append((("pg", ic, tg), 16 * 512 + 128))
                p3_ev.append((("pu", ic, tg), 16 * 512 + 128))
        tot_att = sum(c for _, c in att_ev)
        tot_p3 = sum(c for _, c in p3_ev)
        ratio = tot_p3 / tot_att
        sched = []
        acc_a = acc_p = 0.0
        pi = 0
        for ev, c in att_ev:
            sched.append(ev)
            acc_a += c
            npump = 0
            while pi < len(p3_ev) and acc_p < acc_a * ratio and npump < 3:
                sched.append(p3_ev[pi][0])
                acc_p += p3_ev[pi][1]
                pi += 1
                npump += 1
        while pi < len(p3_ev):
            sched.append(p3_ev[pi][0])
            pi += 1

        GB = {}
        gb_running = 0
        for jq in range(128):
            batches = self.p4_batches(jq % 4)
            GB[jq] = (gb_running, batches)
            gb_running += len(batches)

        def sy(sy):
            s_kv.inc(sy.dma_start(out=kh[0][:],
                                  in_=self.k_scr[SH[0][1]][:, 0:SEQ]), 16)
            s_kv.inc(sy.dma_start(
                out=vh[0][:].rearrange("k (a c) -> k a c", a=16),
                in_=self.v_scr[0, SH[0][1]].rearrange("a k c -> k a c")), 16)
            s_q.inc(sy.dma_start(out=qh[0][:], in_=self.q_scr[SH[0][1]][:, 0:512]), 16)
            for dc in range(16):
                s_xnr.inc(sy.dma_start(
                    out=xnr[dc][:].rearrange("p (g t) -> p g t", g=NTG),
                    in_=self.xn_scr[:, :, dc, :].rearrange("g p t -> p g t")), 16)
            def load_w(ic):
                for buf, w16T in ((gT, self.g16T), (uT, self.u16T)):
                    s_w3.inc(sy.dma_start(
                        out=buf[ic % 2][:].rearrange("p (dc i) -> p dc i", i=128),
                        in_=w16T.rearrange("(dc p) i -> p dc i", p=128)[
                            :, :, ic * 128:(ic + 1) * 128]), 16)

            for ic0 in (0, 1):
                load_w(ic0)
            for ev in sched:
                if ev[0] == "sc":
                    _, jq, qs = ev
                    if qs != 0:
                        continue
                    j, qg = jq // 4, jq % 4
                    if qg == 0 and j + 1 < 32:
                        jn = j + 1
                        s, h = SH[jn]
                        if jn >= 2:
                            s_pv.wait(sy, (jn - 1) * 4)
                        s_kv.inc(sy.dma_start(
                            out=kh[jn % 2][:],
                            in_=self.k_scr[h][:, s * SEQ:(s + 1) * SEQ]), 16)
                        s_kv.inc(sy.dma_start(
                            out=vh[jn % 2][:].rearrange("k (a c) -> k a c", a=16),
                            in_=self.v_scr[s, h].rearrange("a k c -> k a c")), 16)
                    if jq + 1 < 128:
                        jqn = jq + 1
                        jn2, qgn = jqn // 4, jqn % 4
                        s2, h2 = SH[jn2]
                        if jqn >= 2:
                            s_pv.wait(sy, jqn - 1)
                        s_q.inc(sy.dma_start(
                            out=qh[jqn % 2][:],
                            in_=self.q_scr[h2][:, s2 * SEQ + qgn * 512:
                                               s2 * SEQ + (qgn + 1) * 512]), 16)
                elif ev[0] == "pu":
                    _, ic, tg = ev
                    if tg == NTG - 1 and ic + 2 < NIC:
                        s_um.wait(sy, (ic + 1) * 8)
                        load_w(ic + 2)


        def sy_tail(sy):
            self.pc["dw"].wait(sy)
            for k in range(NIC):
                if k >= 1:
                    s_dts.wait(sy, k * 16)
                s_dt.inc(sy.dma_start_transpose(
                    out=dstg[0][:],
                    in_=self.dw16[:, k * 128:(k + 1) * 128]), 16)
                s_dt.wait(sy, (k + 1) * 16)
                s_dts.inc(sy.dma_start(
                    out=self.dw16T[k * 128:(k + 1) * 128, :],
                    in_=dstg[0][:]), 16)
            s_dts.wait(sy)

        def pe(t):
            first_p3 = True
            gb = {}
            for ev in sched:
                if ev[0] == "sc":
                    _, jq, qs = ev
                    j, qg = jq // 4, jq % 4
                    iq = jq * 4 + qs
                    qt = qg * 4 + qs
                    if qs == 0:
                        t.wait_ge(s_kv.sem, (j + 1) * 32)
                        t.wait_ge(s_q.sem, (jq + 1) * 16)
                    soff = (qs % 2) * (qg + 1) * 512 if qg <= 1 else 0
                    if qg <= 1 and qs >= 1:
                        s_ex.wait(t, iq - 1)
                    else:
                        s_ex.wait(t, iq)
                    lhs = qh[jq % 2][:, qs * 128:(qs + 1) * 128]
                    for kc in range(qg):
                        ins = t.matmul(sps[:, soff + kc * 512:soff + (kc + 1) * 512],
                                       lhs,
                                       kh[j % 2][:, kc * 512:(kc + 1) * 512],
                                       start=True, stop=True)
                        if kc == qg - 1:
                            s_sc.inc(ins)
                    bank = sps[:, soff + qg * 512:soff + (qg + 1) * 512]
                    for ks in range(qs + 1):
                        kt = qg * 4 + ks
                        ins = t.matmul(bank[:, ks * 128:(ks + 1) * 128], lhs,
                                       kh[j % 2][:, kt * 128:(kt + 1) * 128],
                                       start=(ks == 0), stop=False)
                        if qg == 0 and ks == qs:
                            s_sc.inc(ins)
                    s_sc.inc(t.matmul(bank[:, qs * 128:(qs + 1) * 128],
                                      self.ident_bf[:], self.maskU[:],
                                      start=False, stop=True))
                    if qs >= 1:
                        s_ex.wait(t, iq)
                        gb0, batches = GB[jq]
                        pos = gb.get(jq, 0)
                        while pos < len(batches) and batches[pos][0] == qs - 1:
                            pqs, kt0, nb = batches[pos]
                            half = ((gb0 + pos) % 2) * 1024
                            if gb0 + pos >= 1:
                                s_te.wait(t, gb0 + pos - 1)
                            for u in range(nb):
                                ins = t.transpose(
                                    tps[:, half + u * 128:half + (u + 1) * 128],
                                    pb[pqs % 2][:, (kt0 + u) * 128:(kt0 + u + 1) * 128],
                                    self.ident_bf[:])
                            s_tp.inc(ins)
                            pos += 1
                        gb[jq] = pos
                elif ev[0] == "t1":
                    jq = ev[1]
                    s_ex.wait(t, jq * 4 + 4)
                    gb0, batches = GB[jq]
                    pos = gb.get(jq, 0)
                    while pos < len(batches):
                        pqs, kt0, nb = batches[pos]
                        half = ((gb0 + pos) % 2) * 1024
                        if gb0 + pos >= 1:
                            s_te.wait(t, gb0 + pos - 1)
                        for u in range(nb):
                            ins = t.transpose(
                                tps[:, half + u * 128:half + (u + 1) * 128],
                                pb[pqs % 2][:, (kt0 + u) * 128:(kt0 + u + 1) * 128],
                                self.ident_bf[:])
                        s_tp.inc(ins)
                        pos += 1
                    gb[jq] = pos
                elif ev[0] == "t2":
                    jq = ev[1]
                    s_ae.wait(t, jq)
                    s_rc.wait(t, 4 * jq + 4)
                    par = 8 + (jq % 2) * 4
                    for qs in range(4):
                        ins = t.transpose(aor[0:1, qs * 128:(qs + 1) * 128],
                                          cols[:, par + qs:par + qs + 1],
                                          self.ident_f[:])
                    s_rr.inc(ins)
                elif ev[0] == "t3":
                    jq = ev[1]
                    s_re.wait(t, jq + 1)
                    s_rb.inc(t.matmul(aor[:], self.ones_col[:], rrow[0:1, :],
                                      start=True, stop=True))
                elif ev[0] == "t4":
                    jq = ev[1]
                    j, qg = jq // 4, jq % 4
                    gb0, batches = GB[jq]
                    s_rbe.wait(t, jq + 1)
                    s_te.wait(t, gb0 + len(batches))
                    for kt in range(4 * qg + 4):
                        ki = kt - 4 * qg
                        lo = max(0, ki) * 128
                        ins = t.matmul(aor[:, lo:512],
                                       vh[j % 2][:, kt * 128:(kt + 1) * 128],
                                       pt[:, kt * 512 + lo:(kt + 1) * 512],
                                       start=(kt == 0), stop=(kt == 4 * qg + 3))
                    s_pv.inc(ins)
                elif ev[0] == "pg":
                    _, ic, tg = ev
                    k = ic * NTG + tg
                    if first_p3:
                        s_xnr.wait(t)
                        first_p3 = False
                    if tg == 0:
                        s_w3.wait(t, (ic + 1) * 2 * 16)
                    s_hd.wait(t, k)
                    for dc in range(16):
                        ins = t.matmul(gps[:], gT[ic % 2][:, dc * 128:(dc + 1) * 128],
                                       xnr[dc][:, tg * 512:(tg + 1) * 512],
                                       start=(dc == 0), stop=(dc == 15))
                    s_gm.inc(ins)
                else:
                    _, ic, tg = ev
                    k = ic * NTG + tg
                    s_sil.wait(t, k + 1)
                    for dc in range(16):
                        ins = t.matmul(ups[:], uT[ic % 2][:, dc * 128:(dc + 1) * 128],
                                       xnr[dc][:, tg * 512:(tg + 1) * 512],
                                       start=(dc == 0), stop=(dc == 15))
                    s_um.inc(ins)

        def ac(a):
            for ev in sched:
                if ev[0] == "sc":
                    _, jq, qs = ev
                    qg = jq % 4
                    iq = jq * 4 + qs
                    L = (qg * 4 + qs + 1) * 128
                    soff = (qs % 2) * (qg + 1) * 512 if qg <= 1 else 0
                    s_mx.wait(a, iq + 1)
                    s_ex.inc(a.activation(pb[qs % 2][:, 0:L], sps[:, soff:soff + L],
                                          AF.Exp, bias=cols[:, qs % 2:qs % 2 + 1],
                                          scale=1.0,
                                          accum_out=cols[:, 4 + qs:5 + qs]))
                elif ev[0] == "pg":
                    _, ic, tg = ev
                    k = ic * NTG + tg
                    s_gm.wait(a, k + 1)
                    if k >= 2:
                        s_hd.wait(a, k - 1)
                    s_sil.inc(a.activation(sil[k % 2][:], gps[:], AF.Silu,
                                           bias=0.0, scale=self.gs_cols[:, ic:ic + 1]))

        def ve(v):
            gb = {}
            for ev in sched:
                if ev[0] == "sc":
                    _, jq, qs = ev
                    qg = jq % 4
                    iq = jq * 4 + qs
                    L = (qg * 4 + qs + 1) * 128
                    par = 8 + (jq % 2) * 4
                    soff = (qs % 2) * (qg + 1) * 512 if qg <= 1 else 0
                    if qg >= 1:
                        s_sc.wait(v, 2 * iq + 1)
                        v.reduce_max(cols[:, 2:3], sps[:, soff:soff + qg * 512],
                                     axis=AX.X, negate=True)
                        s_sc.wait(v, 2 * iq + 2)
                        self.vs(v, v.reduce_max(cols[:, 3:4],
                                                sps[:, soff + qg * 512:soff + L],
                                                axis=AX.X, negate=True))
                        s_mx.inc(v.tensor_tensor(cols[:, qs % 2:qs % 2 + 1],
                                                 cols[:, 2:3],
                                                 cols[:, 3:4], ALU.min))
                    else:
                        s_sc.wait(v, 2 * iq + 2)
                        s_mx.inc(v.reduce_max(cols[:, qs % 2:qs % 2 + 1],
                                              sps[:, soff:soff + L],
                                              axis=AX.X, negate=True))
                    s_ex.wait(v, iq + 1)
                    s_rc.inc(v.reciprocal(cols[:, par + qs:par + qs + 1],
                                          cols[:, 4 + qs:5 + qs]))
                    if qs >= 1:
                        gb0, batches = GB[jq]
                        pos = gb.get(jq, 0)
                        while pos < len(batches) and batches[pos][0] == qs - 1:
                            pqs, kt0, nb = batches[pos]
                            s_tp.wait(v, gb0 + pos + 1)
                            half = ((gb0 + pos) % 2) * 1024
                            dst = pt[:].rearrange("p (kt q) -> p kt q", kt=16)[
                                :, kt0:kt0 + nb, pqs * 128:(pqs + 1) * 128]
                            src = tps[:, half:half + nb * 128].rearrange(
                                "p (a b) -> p a b", b=128)
                            s_te.inc(v.tensor_copy(dst, src))
                            pos += 1
                        gb[jq] = pos
                elif ev[0] == "t1":
                    jq = ev[1]
                    gb0, batches = GB[jq]
                    pos = gb.get(jq, 0)
                    while pos < len(batches):
                        pqs, kt0, nb = batches[pos]
                        s_tp.wait(v, gb0 + pos + 1)
                        half = ((gb0 + pos) % 2) * 1024
                        dst = pt[:].rearrange("p (kt q) -> p kt q", kt=16)[
                            :, kt0:kt0 + nb, pqs * 128:(pqs + 1) * 128]
                        src = tps[:, half:half + nb * 128].rearrange(
                            "p (a b) -> p a b", b=128)
                        s_te.inc(v.tensor_copy(dst, src))
                        pos += 1
                    gb[jq] = pos
                elif ev[0] == "t2":
                    jq = ev[1]
                    s_rr.wait(v, jq + 1)
                    s_re.inc(v.tensor_copy(rrow[0:1, :], aor[0:1, 0:512]))
                elif ev[0] == "t3":
                    jq = ev[1]
                    s_rb.wait(v, jq + 1)
                    s_rbe.inc(v.tensor_copy(rbc[:], aor[:]))
                    s_rbe.wait(v)
                elif ev[0] == "t4":
                    jq = ev[1]
                    s_pv.wait(v, jq + 1)
                    if jq >= 2:
                        s_sta.wait(v, (jq - 1) * 16)
                    s_ae.inc(v.scalar_tensor_tensor(aoe[jq % 2][:], aor[:], 1.0,
                                                    rbc[:], ALU.bypass, ALU.mult))
                elif ev[0] == "pu":
                    _, ic, tg = ev
                    k = ic * NTG + tg
                    s_um.wait(v, k + 1)
                    s_sil.wait(v, k + 1)
                    if k >= 2:
                        s_st3.wait(v, (k - 1) * 16)
                    s_hd.inc(v.scalar_tensor_tensor(
                        hbf[k % 2][:], ups[:], self.us_cols[:, ic:ic + 1],
                        sil[k % 2][:], ALU.mult, ALU.mult))

        def gp(g):
            for src_t, dst, nrows, width, ctr, step in (
                    (self.down_w, self.dw16, D, INTER, self.pc["dw"], 256),
                    (self.w_out, self.wo16, D, D, self.pc["wo"], 512)):
                for r0 in range(0, nrows, step):
                    r1 = min(r0 + step, nrows)
                    ctr.inc(g.dma_start(out=dst[r0:r1, :], in_=src_t[r0:r1, :]), 16)
            for ev in sched:
                if ev[0] == "t4":
                    jq = ev[1]
                    if jq >= 1:
                        s_ae.wait(g, jq)
                        s_sta.inc(g.dma_start(out=self._ao_dst(jq - 1, SH),
                                              in_=aoe[(jq - 1) % 2][:]), 16)
                elif ev[0] == "pu":
                    _, ic, tg = ev
                    k = ic * NTG + tg
                    s_hd.wait(g, k + 1)
                    s_st3.inc(g.dma_start(
                        out=self.hdn_scr[ic][:, tg * 512:(tg + 1) * 512],
                        in_=hbf[k % 2][:]), 16)
            s_ae.wait(g, 128)
            s_sta.inc(g.dma_start(out=self._ao_dst(127, SH), in_=aoe[127 % 2][:]), 16)
            s_sta.wait(g)
            s_st3.wait(g)

        self.blk.sync(sy)
        self.blk.sync(sy_tail)
        self.blk.tensor(pe)
        self.blk.scalar(ac)
        self.blk.vector(ve)
        self.blk.gpsimd(gp)
        self.emit_barrier()
        stk.close()

    def _ao_dst(self, jq, SH):
        j, qg = jq // 4, jq % 4
        s, h = SH[j]
        return self.ao_scr[h][:, s * SEQ + qg * 512:s * SEQ + (qg + 1) * 512]

    def ph_p5(self):
        stk = ExitStack()
        dwT = [self.sb(stk, f"p5_dw{ic}", [128, 1024], BF16) for ic in range(NIC)]
        slab = [self.sb(stk, f"p5_sl{i}", [128, NIC * 128], BF16) for i in range(2)]
        fev = [self.sb(stk, f"p5_f{i}", [128, 512], BF16) for i in range(2)]
        dps = [self.psum(stk, f"p5_dp{i}", [128, 512], F32) for i in range(2)]

        s_w = self.ctr("p5_w")
        s_sl = self.ctr("p5_sl")
        s_dm = self.ctr("p5_dm")
        s_de = self.ctr("p5_de")
        s_st = self.ctr("p5_st")
        s_hf = self.ctr("p5_hf")

        def sy(sy):
            for half in range(2):
                if half == 1:
                    s_dm.wait(sy, 64)
                for ic in range(NIC):
                    s_w.inc(sy.dma_start(
                        out=dwT[ic][:],
                        in_=self.dw16T[ic * 128:(ic + 1) * 128,
                                       half * 1024:(half + 1) * 1024]), 16)
                for tt in range(NTT):
                    i = half * NTT + tt
                    if i >= 2:
                        s_dm.wait(sy, (i - 1) * 2)
                    for b in range(4):
                        nic = 11
                        s_sl.inc(sy.dma_start(
                            out=slab[i % 2][:, b * nic * 128:(b + 1) * nic * 128].rearrange(
                                "p (a t) -> p a t", a=nic),
                            in_=self.hdn_scr[b * nic:(b + 1) * nic, :,
                                             tt * 128:(tt + 1) * 128].rearrange(
                                "a p t -> p a t")), 16)

        def pe(t):
            for half in range(2):
                s_w.wait(t, (half + 1) * NIC * 16)
                for tt in range(NTT):
                    i = half * NTT + tt
                    s_sl.wait(t, (i + 1) * 64)
                    for hb in range(2):
                        k = i * 2 + hb
                        if k >= 2:
                            s_de.wait(t, k - 1)
                        for ic in range(NIC):
                            ins = t.matmul(dps[k % 2][:],
                                           slab[i % 2][:, ic * 128:(ic + 1) * 128],
                                           dwT[ic][:, hb * 512:(hb + 1) * 512],
                                           start=(ic == 0), stop=(ic == NIC - 1))
                        s_dm.inc(ins)

        def ve(v):
            for half in range(2):
                for tt in range(NTT):
                    for hb in range(2):
                        k = (half * NTT + tt) * 2 + hb
                        s_dm.wait(v, k + 1)
                        if k >= 2:
                            s_st.wait(v, (k - 1) * 16)
                        s_de.inc(v.tensor_copy(fev[k % 2][:], dps[k % 2][:]))

        def gp(g):
            for half in range(2):
                for tt in range(NTT):
                    for hb in range(2):
                        k = (half * NTT + tt) * 2 + hb
                        s_de.wait(g, k + 1)
                        s_st.inc(g.dma_start(
                            out=self.ffn_scr[tt][:, half * 1024 + hb * 512:
                                                half * 1024 + (hb + 1) * 512],
                            in_=fev[k % 2][:]), 16)
            s_st.wait(g)

        self.blk.sync(sy)
        self.blk.tensor(pe)
        self.blk.vector(ve)
        self.blk.gpsimd(gp)
        self.blk.scalar(lambda a: None)
        self.emit_barrier()
        stk.close()

    def ph_p6(self):
        stk = ExitStack()
        woT = [self.sb(stk, f"p6_wo{dc}", [128, D], BF16) for dc in range(16)]
        aog = [[self.sb(stk, f"p6_a{i}_{hh}", [128, 512], BF16) for hh in range(16)]
               for i in range(2)]
        xb = [self.sb(stk, f"p6_x{i}", [128, D], F32) for i in range(2)]
        fb = [self.sb(stk, f"p6_f{i}", [128, D], BF16) for i in range(2)]
        yb = [self.sb(stk, f"p6_y{i}", [128, D], F32) for i in range(2)]
        acc1 = [self.sb(stk, f"p6_c1{i}", [128, D], F32) for i in range(2)]
        acc2 = self.sb(stk, "p6_c2", [128, D], F32)
        borow = self.sb(stk, "p6_bor", [1, D], F32)
        dsrow = self.sb(stk, "p6_dsr", [1, D], F32)
        bo_bc = self.sb(stk, "p6_bobc", [128, D], F32)
        ds_bc = self.sb(stk, "p6_dsbc", [128, D], F32)
        prps = [self.psum(stk, f"p6_pp{i}", [128, D], F32) for i in range(2)]

        s_r = self.ctr("p6_r")
        s_bm = self.ctr("p6_bm")
        s_be = self.ctr("p6_be")
        s_wo = self.ctr("p6_wo")
        s_ao = self.ctr("p6_ao")
        s_x = self.ctr("p6_x")
        s_f = self.ctr("p6_f")
        s_pm = self.ctr("p6_pm")
        s_ye = self.ctr("p6_ye")
        s_st = self.ctr("p6_st")

        def sy(sy):
            s_r.inc(sy.dma_start(out=borow[:], in_=self.b_out[None, :]), 16)
            s_r.inc(sy.dma_start(out=dsrow[:], in_=self.down_s[None, :]), 16)
            self.pc["wo"].wait(sy)
            for dc in range(16):
                s_wo.inc(sy.dma_start_transpose(
                    out=woT[dc][:], in_=self.wo16[:, dc * 128:(dc + 1) * 128]), 16)
            for g in range(NTG):
                if g >= 2:
                    s_pm.wait(sy, (g - 1) * 4)
                for hh in range(16):
                    s_ao.inc(sy.dma_start(
                        out=aog[g % 2][hh][:],
                        in_=self.ao_scr[hh][:, g * 512:(g + 1) * 512]), 16)
                for ts in range(4):
                    tt = g * 4 + ts
                    s = tt // 16
                    r = (tt % 16) * 128
                    if tt >= 2:
                        s_ye.wait(sy, tt - 1)
                    s_x.inc(sy.dma_start(out=xb[tt % 2][:],
                                         in_=self.x[s, r:r + 128, :]), 16)
                    s_f.inc(sy.dma_start(out=fb[tt % 2][:], in_=self.ffn_scr[tt]), 16)

        def pe(t):
            s_r.wait(t)
            for b in range(8):
                if b >= 1:
                    s_be.wait(t, b)
                row = borow if b < 4 else dsrow
                s_bm.inc(t.matmul(prps[0][:, 0:512], self.ones_col[:],
                                  row[0:1, (b % 4) * 512:(b % 4 + 1) * 512],
                                  start=True, stop=True))
            s_wo.wait(t)
            s_be.wait(t, 8)
            for tt in range(NTT):
                g, ts = tt // 4, tt % 4
                s_ao.wait(t, (g + 1) * 256)
                if tt >= 2:
                    s_ye.wait(t, tt - 1)
                for eb in range(4):
                    for dc in range(16):
                        ins = t.matmul(prps[tt % 2][:, eb * 512:(eb + 1) * 512],
                                       aog[g % 2][dc][:, ts * 128:(ts + 1) * 128],
                                       woT[dc][:, eb * 512:(eb + 1) * 512],
                                       start=(dc == 0), stop=(dc == 15))
                s_pm.inc(ins)

        def ve(v):
            for b in range(8):
                s_bm.wait(v, b + 1)
                dst = bo_bc if b < 4 else ds_bc
                s_be.inc(v.tensor_copy(dst[:, (b % 4) * 512:(b % 4 + 1) * 512],
                                       prps[0][:, 0:512]))
            for tt in range(NTT):
                s_f.wait(v, (tt + 1) * 16)
                s_x.wait(v, (tt + 1) * 16)
                v.scalar_tensor_tensor(acc1[tt % 2][:], fb[tt % 2][:], 1.0,
                                       ds_bc[:], ALU.bypass, ALU.mult)
                self.vs(v, v.scalar_tensor_tensor(acc2[:], xb[tt % 2][:], 1.0,
                                                  bo_bc[:], ALU.bypass, ALU.add))
                self.vs(v, v.tensor_tensor(acc1[tt % 2][:], acc1[tt % 2][:],
                                           acc2[:], ALU.add))
                s_pm.wait(v, tt + 1)
                if tt >= 2:
                    s_st.wait(v, (tt - 1) * 16)
                s_ye.inc(v.scalar_tensor_tensor(yb[tt % 2][:], prps[tt % 2][:], 1.0,
                                                acc1[tt % 2][:], ALU.bypass, ALU.add))

        def gp(g):
            for tt in range(NTT):
                s = tt // 16
                r = (tt % 16) * 128
                s_ye.wait(g, tt + 1)
                s_st.inc(g.dma_start(out=self.y[s, r:r + 128, :], in_=yb[tt % 2][:]), 16)
            s_st.wait(g)

        self.blk.sync(sy)
        self.blk.tensor(pe)
        self.blk.vector(ve)
        self.blk.gpsimd(gp)
        self.blk.scalar(lambda a: None)
        stk.close()


def build_nc(dbg=()):
    return B(dbg=dbg).build()


_NC_CACHE = {}


def kernel(**inputs):
    key = "main"
    if key not in _NC_CACHE:
        _NC_CACHE[key] = build_nc()
    nc = _NC_CACHE[key]
    x = np.ascontiguousarray(inputs["L0_attn_x"])
    base = {
        "w_qkv": np.ascontiguousarray(inputs["L0_attn_w_qkv"]),
        "b_qkv": np.ascontiguousarray(inputs["L0_attn_b_qkv"]),
        "w_out": np.ascontiguousarray(inputs["L0_attn_w_out"]),
        "b_out": np.ascontiguousarray(inputs["L0_attn_b_out"]),
        "gate_w": np.ascontiguousarray(inputs["L0_ffn_gate_w"]),
        "up_w": np.ascontiguousarray(inputs["L0_ffn_up_w"]),
        "down_w": np.ascontiguousarray(inputs["L0_ffn_down_w"]),
        "gate_s": np.ascontiguousarray(inputs["L0_ffn_gate_s"]),
        "up_s": np.ascontiguousarray(inputs["L0_ffn_up_s"]),
        "down_s": np.ascontiguousarray(inputs["L0_ffn_down_s"]),
    }
    in_maps = [dict(base, x=np.ascontiguousarray(x[2 * i:2 * i + 2]))
               for i in range(8)]
    res = run_bass_kernel_spmd(nc, in_maps, core_ids=list(range(8)))
    return np.concatenate([r["y"] for r in res.results], axis=0)


# version 24
# speedup vs baseline: 1.0679x; 1.0679x over previous
import math
from contextlib import ExitStack

import numpy as np

import concourse.bass as bass
import concourse.mybir as mybir
from concourse.bass_utils import run_bass_kernel_spmd

F32 = mybir.dt.float32
BF16 = mybir.dt.bfloat16
AF = mybir.ActivationFunctionType
ALU = mybir.AluOpType
AX = mybir.AxisListType

D = 2048
H = 16
HD = 128
INTER = 5632
SEQ = 2048
NSEQ = 2
T = NSEQ * SEQ
NTT = T // 128
NTG = T // 512
NIC = INTER // 128
NIG2 = INTER // 256
RS = 1.0 / math.sqrt(HD)
EPS = 1e-5
NEGINF = -1e9


class Ctr:

    def __init__(self, nc, name):
        self.sem = nc.alloc_semaphore(name)
        self.n = 0

    def inc(self, inst, k=1):
        inst.then_inc(self.sem, k)
        self.n += k
        return self.n

    def wait(self, eng, n=None):
        eng.wait_ge(self.sem, self.n if n is None else n)


class B:
    def __init__(self, dbg=()):
        self.nc = bass.Bass("TRN2", target_bir_lowering=False)
        self.dbg = set(dbg)
        self.ctrs = {}
        self.stack = ExitStack()
        self.bar_ctr = Ctr(self.nc, "bar")
        self.bar_round = {}
        nc = self.nc

        self.dbg_pairs = []

        def dram(name, shape, dtype, kind="Internal"):
            ap = nc.dram_tensor(name, shape, dtype, kind=kind).ap()
            if name in self.dbg:
                dbg_ap = nc.dram_tensor(name + "__dbg", shape, F32,
                                        kind="ExternalOutput").ap()
                self.dbg_pairs.append((ap, dbg_ap))
            return ap

        self.x = nc.dram_tensor("x", [NSEQ, SEQ, D], F32, kind="ExternalInput").ap()
        self.w_qkv = nc.dram_tensor("w_qkv", [3 * D, D], F32, kind="ExternalInput").ap()
        self.b_qkv = nc.dram_tensor("b_qkv", [3 * D], F32, kind="ExternalInput").ap()
        self.w_out = nc.dram_tensor("w_out", [D, D], F32, kind="ExternalInput").ap()
        self.b_out = nc.dram_tensor("b_out", [D], F32, kind="ExternalInput").ap()
        self.gate_w = nc.dram_tensor("gate_w", [INTER, D], F32, kind="ExternalInput").ap()
        self.up_w = nc.dram_tensor("up_w", [INTER, D], F32, kind="ExternalInput").ap()
        self.down_w = nc.dram_tensor("down_w", [D, INTER], F32, kind="ExternalInput").ap()
        self.gate_s = nc.dram_tensor("gate_s", [INTER], F32, kind="ExternalInput").ap()
        self.up_s = nc.dram_tensor("up_s", [INTER], F32, kind="ExternalInput").ap()
        self.down_s = nc.dram_tensor("down_s", [D], F32, kind="ExternalInput").ap()
        self.y = nc.dram_tensor("y", [NSEQ, SEQ, D], F32, kind="ExternalOutput").ap()

        self.wq16 = dram("wq16", [D, D], BF16)
        self.wk16 = dram("wk16", [D, D], BF16)
        self.wv16 = dram("wv16", [D, D], BF16)
        self.g16 = dram("g16", [INTER, D], BF16)
        self.u16 = dram("u16", [INTER, D], BF16)
        self.dw16 = dram("dw16", [D, INTER], BF16)
        self.g16T = dram("g16T", [D, INTER], BF16)
        self.u16T = dram("u16T", [D, INTER], BF16)
        self.dw16T = dram("dw16T", [INTER, D], BF16)
        self.wo16 = dram("wo16", [D, D], BF16)
        self.wo16T = dram("wo16T", [D, D], BF16)

        self.xn_scr = dram("xn_scr", [NTG, 128, 16, 512], BF16)
        self.q_scr = dram("q_scr", [H, 128, T], BF16)
        self.k_scr = dram("k_scr", [H, 128, T], BF16)
        self.v_scr = dram("v_scr", [NSEQ, H, 16, 128, 128], BF16)
        self.hdn_scr = dram("hdn_scr", [NIC, 128, T], BF16)
        self.ao_scr = dram("ao_scr", [H, 128, T], BF16)
        self.ffn_scr = dram("ffn_scr", [NTT, 128, D], BF16)

    def ctr(self, name):
        if name not in self.ctrs:
            self.ctrs[name] = Ctr(self.nc, name)
        return self.ctrs[name]

    def sb(self, stk, name, shape, dtype):
        return stk.enter_context(self.nc.sbuf_tensor(name, shape, dtype))

    def psum(self, stk, name, shape, dtype=F32):
        return stk.enter_context(self.nc.psum_tensor(name, shape, dtype))

    def vs(self, eng, inst):
        c = self.ctr("self_" + str(eng.engine))
        c.inc(inst)
        c.wait(eng)

    def _wtr(self, sy, k, wstg, s_gt, s_gts, HI):
        if k < 64:
            if k >= 2:
                s_gts.wait(sy, (k - 1) * 16)
            w16 = self.g16 if k < 32 else self.u16
            dc, piece = (k % 32) // 2, k % 2
            s_gt.inc(sy.dma_start_transpose(
                out=wstg[k % 2][:],
                in_=w16[piece * HI:(piece + 1) * HI,
                        dc * 128:(dc + 1) * 128]), 16)
        if k >= 1:
            ks = k - 1
            w16T = self.g16T if ks < 32 else self.u16T
            dc, piece = (ks % 32) // 2, ks % 2
            s_gt.wait(sy, (ks + 1) * 16)
            s_gts.inc(sy.dma_start(
                out=w16T[dc * 128:(dc + 1) * 128,
                         piece * HI:(piece + 1) * HI],
                in_=wstg[ks % 2][:]), 16)

    def _wtr_end(self):
        pass

    def wbuild(self, tag, src_rows, wT_all, fstg, bstg, wtp, base=0):
        ld = self.ctr(tag + "_ld")
        ca = self.ctr(tag + "_ca")
        tr = self.ctr(tag + "_tr")
        ev = self.ctr(tag + "_ev")

        def sy_emit(sy):
            for c in range(32):
                ec, half = c // 2, c % 2
                if base + c >= 2:
                    ca.wait(sy, base + c - 1)
                ld.inc(sy.dma_start(
                    out=fstg[c % 2][:],
                    in_=src_rows[ec * 128:(ec + 1) * 128,
                                 half * 1024:(half + 1) * 1024]), 16)

        def ac_emit(a):
            for c in range(32):
                ld.wait(a, (base + c + 1) * 16)
                if base + c >= 2:
                    tr.wait(a, base + c - 1)
                ca.inc(a.activation(bstg[c % 2][:], fstg[c % 2][:], AF.Identity,
                                    bias=0.0, scale=1.0))

        def pe_emit(t):
            for c in range(32):
                ec, half = c // 2, c % 2
                ca.wait(t, base + c + 1)
                if base + c >= 2:
                    ev.wait(t, base + c - 1)
                hoff = (c % 2) * 1024
                for d2 in range(8):
                    ins = t.transpose(wtp[:, hoff + d2 * 128:hoff + (d2 + 1) * 128],
                                      bstg[c % 2][:, d2 * 128:(d2 + 1) * 128],
                                      self.ident_bf[:])
                tr.inc(ins)

        def ve_emit(v):
            for c in range(32):
                ec, half = c // 2, c % 2
                tr.wait(v, base + c + 1)
                hoff = (c % 2) * 1024
                dst = wT_all[:].rearrange("p (dc e) -> p dc e", e=2048)[
                    :, half * 8:half * 8 + 8, ec * 128:(ec + 1) * 128]
                ev.inc(v.tensor_copy(
                    dst, wtp[:, hoff:hoff + 1024].rearrange("p (a b) -> p a b", b=128)))

        return sy_emit, ac_emit, pe_emit, ve_emit, ev

    def bar(self, eng, who):
        r = self.bar_round.get(who, 0) + 1
        self.bar_round[who] = r
        eng.drain()
        eng.nop().then_inc(self.bar_ctr.sem, 1)
        eng.wait_ge(self.bar_ctr.sem, 5 * r)

    def build(self):
        nc = self.nc
        cst = self.stack
        self.ident_bf = self.sb(cst, "ident_bf", [128, 128], BF16)
        self.ident_f = self.sb(cst, "ident_f", [128, 128], F32)
        self.maskU = self.sb(cst, "maskU", [128, 128], BF16)
        self.ones_col = self.sb(cst, "ones_col", [1, 128], F32)
        self.bq_cols = self.sb(cst, "bq_cols", [128, 32], F32)
        self.bqs_cols = self.sb(cst, "bqs_cols", [128, 16], F32)
        self.gs_cols = self.sb(cst, "gs_cols", [128, NIC], F32)
        self.us_cols = self.sb(cst, "us_cols", [128, NIC], F32)

        with nc.Block() as block:
            self.blk = block
            self.ph_consts()
            self.ph_p1()
            self.ph_p2()
            self.ph_p34()
            self.ph_p5()
            self.ph_p6()
            if self.dbg_pairs:
                self.emit_barrier()
                dbgc = self.ctr("dbgc")

                def gp_dbg(g):
                    for ap, dbg_ap in self.dbg_pairs:
                        dbgc.inc(g.dma_start(out=dbg_ap, in_=ap), 16)
                    dbgc.wait(g)
                self.blk.gpsimd(gp_dbg)
        self.stack.close()
        return nc

    def ph_consts(self):
        stk = ExitStack()
        bqr = self.sb(stk, "c_bqr", [128, 128], F32)
        gsr = self.sb(stk, "c_gsr", [128, 128], F32)
        usr = self.sb(stk, "c_usr", [128, 128], F32)
        tp = self.psum(stk, "c_tp", [128, 128], F32)

        c_ld = self.ctr("c_ld")
        c_g = self.ctr("c_g")
        c_pe = self.ctr("c_pe")
        c_v = self.ctr("c_v")
        pc = {w: self.ctr("pc_" + w) for w in ("wq", "wk", "wv", "g", "u", "dw", "wo")}
        self.pc = pc

        def gp(g):
            self.vs(g, g.memset(self.ident_bf[:], 0.0))
            g.affine_select(out=self.ident_bf[:], in_=self.ident_bf[:],
                            compare_op=ALU.not_equal, fill=1.0, base=0,
                            pattern=[[-1, 128]], channel_multiplier=1)
            self.vs(g, g.memset(self.ident_f[:], 0.0))
            g.affine_select(out=self.ident_f[:], in_=self.ident_f[:],
                            compare_op=ALU.not_equal, fill=1.0, base=0,
                            pattern=[[-1, 128]], channel_multiplier=1)
            self.vs(g, g.memset(self.maskU[:], 0.0))
            g.affine_select(out=self.maskU[:], in_=self.maskU[:],
                            compare_op=ALU.is_ge, fill=NEGINF, base=0,
                            pattern=[[-1, 128]], channel_multiplier=1)
            c_g.inc(g.memset(self.ones_col[:], 1.0))
            g.memset(bqr[:], 0.0)
            g.memset(gsr[:], 0.0)
            c_g.inc(g.memset(usr[:], 0.0))

        def sy(sy):
            c_g.wait(sy, 2)
            c_ld.inc(sy.dma_start(out=bqr[0:48, :],
                                  in_=self.b_qkv.rearrange("(c p) -> c p", p=128)), 16)
            c_ld.inc(sy.dma_start(out=gsr[0:44, :],
                                  in_=self.gate_s.rearrange("(c p) -> c p", p=128)), 16)
            c_ld.inc(sy.dma_start(out=usr[0:44, :],
                                  in_=self.up_s.rearrange("(c p) -> c p", p=128)), 16)

        def pe(t):
            c_ld.wait(t)
            c_g.wait(t)
            t2 = t.transpose(tp[:, :], bqr[:, :], self.ident_f[:])
            c_pe.inc(t2)
            c_v.wait(t, 1)
            c_pe.inc(t.transpose(tp[:, :], gsr[:, :], self.ident_f[:]))
            c_v.wait(t, 2)
            c_pe.inc(t.transpose(tp[:, :], usr[:, :], self.ident_f[:]))

        def ve(v):
            c_pe.wait(v, 1)
            ins = v.tensor_copy(self.bq_cols[:, :], tp[:, 0:32])
            c_v.inc(ins)
            c_v.wait(v)
            v.tensor_scalar_mul(self.bqs_cols[:, :], self.bq_cols[:, 0:16], RS)
            c_pe.wait(v, 2)
            c_v.inc(v.tensor_copy(self.gs_cols[:, :], tp[:, 0:44]))
            c_pe.wait(v, 3)
            c_v.inc(v.tensor_copy(self.us_cols[:, :], tp[:, 0:44]))

        def ac(a):
            pass

        self.blk.gpsimd(gp)
        self.blk.sync(sy)
        self.blk.tensor(pe)
        self.blk.vector(ve)
        self.blk.scalar(ac)
        self.emit_barrier()
        stk.close()

    def emit_barrier(self):
        self.blk.gpsimd(lambda g: self.bar(g, "g"))
        self.blk.sync(lambda s: self.bar(s, "s"))
        self.blk.tensor(lambda t: self.bar(t, "t"))
        self.blk.vector(lambda v: self.bar(v, "v"))
        self.blk.scalar(lambda a: self.bar(a, "a"))

    def ph_p1(self):
        stk = ExitStack()
        wqT_all = self.sb(stk, "p1_wqT", [128, 16 * D], BF16)
        fstg = [self.sb(stk, f"p1_wf{i}", [128, 1024], F32) for i in range(2)]
        bstg = [self.sb(stk, f"p1_wb{i}", [128, 1024], BF16) for i in range(2)]
        xb = [self.sb(stk, f"p1_x{i}", [128, D], F32) for i in range(2)]
        xnb = [self.sb(stk, f"p1_xn{i}", [128, D], BF16) for i in range(4)]
        sq = self.sb(stk, "p1_sq", [128, D], F32)
        st = [self.sb(stk, f"p1_st{i}", [128, 8], F32) for i in range(2)]
        xng = [self.sb(stk, f"p1_xng{i}", [128, 16 * 512], BF16) for i in range(2)]
        qev = [self.sb(stk, f"p1_qe{i}", [128, 512], BF16) for i in range(2)]
        trp = [self.psum(stk, f"p1_tr{i}", [128, D], BF16) for i in range(2)]
        qps = [self.psum(stk, f"p1_qp{i}", [128, 512], F32) for i in range(3)]

        s_x = self.ctr("p1_x")
        s_va = self.ctr("p1_va")
        s_ac = self.ctr("p1_ac")
        s_vb = self.ctr("p1_vb")
        s_ad = self.ctr("p1_ad")
        s_vc = self.ctr("p1_vc")
        s_ae = self.ctr("p1_ae")
        s_tr = self.ctr("p1_tr")
        s_ev = self.ctr("p1_ev")
        s_qm = self.ctr("p1_qm")
        s_qe = self.ctr("p1_qe")
        s_stx = self.ctr("p1_stx")
        s_stq = self.ctr("p1_stq")
        wb_sy, wb_ac, wb_pe, wb_ve, w1_ev = self.wbuild(
            "w1", self.w_qkv[0:D], wqT_all, fstg, bstg, trp[0][:])

        def sy(sy):
            for i in range(2):
                s_x.inc(sy.dma_start(out=xb[i % 2][:], in_=self.x[0, i * 128:(i + 1) * 128, :]), 16)
            wb_sy(sy)
            for i in range(2, NTT):
                s, r = i // 16, (i % 16) * 128
                s_ae.wait(sy, i - 1)
                s_x.inc(sy.dma_start(out=xb[i % 2][:], in_=self.x[s, r:r + 128, :]), 16)

        def ve(v):
            wb_ve(v)
            for g in range(NTG):
                for j in range(4):
                    i = g * 4 + j
                    c = st[i % 2]
                    s_x.wait(v, (i + 1) * 16)
                    if i >= 2:
                        s_ae.wait(v, i - 1)
                    self.vs(v, v.reduce_sum(c[:, 0:1], xb[i % 2][:], axis=AX.X))
                    s_va.inc(v.tensor_scalar_mul(c[:, 1:2], c[:, 0:1], -1.0 / D))
                    s_ac.wait(v, i + 1)
                    s_vb.inc(v.tensor_scalar(c[:, 3:4], c[:, 2:3], 1.0 / D, EPS,
                                             ALU.mult, ALU.add))
                    s_ad.wait(v, i + 1)
                    self.vs(v, v.reciprocal(c[:, 5:6], c[:, 4:5]))
                    s_vc.inc(v.tensor_tensor(c[:, 6:7], c[:, 1:2], c[:, 5:6], ALU.mult))
                if g >= 1:
                    qe_group(v, g - 1)
                for j in range(4):
                    i = g * 4 + j
                    off = j * 128
                    if j == 0 and g >= 2:
                        s_qm.wait(v, (g - 1) * 16)
                        s_stx.wait(v, (g - 1) * 16)
                    s_tr.wait(v, i + 1)
                    dst = xng[g % 2][:].rearrange("p (dc t) -> p dc t", dc=16)[:, :, off:off + 128]
                    s_ev.inc(v.tensor_copy(
                        dst, trp[i % 2][:].rearrange("p (dc t) -> p dc t", dc=16)))
            qe_group(v, NTG - 1)

        def qe_group(v, g):
            for ec in range(16):
                k = g * 16 + ec
                s_qm.wait(v, k + 1)
                if k >= 2:
                    s_stq.wait(v, (k - 1) * 16)
                s_qe.inc(v.tensor_scalar(qev[k % 2][:], qps[k % 3][:],
                                         self.bq_cols[:, ec:ec + 1], RS,
                                         ALU.add, ALU.mult))

        def ac(a):
            wb_ac(a)
            for g in range(NTG):
                for j in range(4):
                    i = g * 4 + j
                    c = st[i % 2]
                    s_va.wait(a, i + 1)
                    s_ac.inc(a.activation(sq[:], xb[i % 2][:], AF.Square,
                                          bias=c[:, 1:2], scale=1.0, accum_out=c[:, 2:3]))
                    s_vb.wait(a, i + 1)
                    s_ad.inc(a.activation(c[:, 4:5], c[:, 3:4], AF.Sqrt))
                    s_vc.wait(a, i + 1)
                    if i >= 4:
                        s_tr.wait(a, i - 3)
                    s_ae.inc(a.activation(xnb[i % 4][:], xb[i % 2][:], AF.Identity,
                                          bias=c[:, 6:7], scale=c[:, 5:6]))

        def pe(t):
            wb_pe(t)
            w1_ev.wait(t, 32)
            for g in range(NTG):
                for j in range(4):
                    i = g * 4 + j
                    s_ae.wait(t, i + 1)
                    if i >= 2:
                        s_ev.wait(t, i - 1)
                    for dc in range(16):
                        ins = t.transpose(trp[i % 2][:, dc * 128:(dc + 1) * 128],
                                          xnb[i % 4][:, dc * 128:(dc + 1) * 128],
                                          self.ident_bf[:])
                    s_tr.inc(ins)
                s_ev.wait(t, (g + 1) * 4)
                for ec in range(16):
                    k = g * 16 + ec
                    if k >= 3:
                        s_qe.wait(t, k - 2)
                    for dc in range(16):
                        ins = t.matmul(
                            qps[k % 3][:],
                            wqT_all[:, dc * 2048 + ec * 128:dc * 2048 + (ec + 1) * 128],
                            xng[g % 2][:, dc * 512:(dc + 1) * 512],
                            start=(dc == 0), stop=(dc == 15))
                    s_qm.inc(ins)

        def gp(g):
            for src, dst, ctr in ((self.gate_w, self.g16, self.pc["g"]),
                                  (self.up_w, self.u16, self.pc["u"])):
                for r0 in range(0, INTER, 512):
                    r1 = min(r0 + 512, INTER)
                    ctr.inc(g.dma_start(out=dst[r0:r1, :], in_=src[r0:r1, :]), 16)
            for gi in range(NTG):
                s_ev.wait(g, (gi + 1) * 4)
                s_stx.inc(g.dma_start(
                    out=self.xn_scr[gi].rearrange("p dc t -> p (dc t)"),
                    in_=xng[gi % 2][:]), 16)
                for ec in range(16):
                    k = gi * 16 + ec
                    s_qe.wait(g, k + 1)
                    s_stq.inc(g.dma_start(
                        out=self.q_scr[ec][:, gi * 512:(gi + 1) * 512],
                        in_=qev[k % 2][:]), 16)
            s_stx.wait(g)
            s_stq.wait(g)

        self.blk.sync(sy)
        self.blk.vector(ve)
        self.blk.scalar(ac)
        self.blk.tensor(pe)
        self.blk.gpsimd(gp)
        self.emit_barrier()
        stk.close()

    def ph_p2(self):
        stk = ExitStack()
        wkT_all = self.sb(stk, "p2_wkT", [128, 16 * D], BF16)
        wvT_all = self.sb(stk, "p2_wvT", [128, 16 * D], BF16)
        fstg = [self.sb(stk, f"p2_wf{i}", [128, 1024], F32) for i in range(2)]
        bstg = [self.sb(stk, f"p2_wb{i}", [128, 1024], BF16) for i in range(2)]
        xng = [self.sb(stk, f"p2_xg{i}", [128, 16 * 512], BF16) for i in range(2)]
        kev = [self.sb(stk, f"p2_ke{i}", [128, 512], BF16) for i in range(2)]
        vev = [self.sb(stk, f"p2_ve{i}", [128, 512], BF16) for i in range(2)]
        bvrow = self.sb(stk, "p2_bvr", [1, D], F32)
        bv_bc = self.sb(stk, "p2_bvbc", [128, D], F32)
        wstg = [self.sb(stk, f"p2_ws{i}", [128, INTER // 2], BF16) for i in range(2)]
        kps = [self.psum(stk, f"p2_kp{i}", [128, 512], F32) for i in range(3)]
        vps = [self.psum(stk, f"p2_vp{i}", [128, 512], F32) for i in range(2)]
        bcp = self.psum(stk, "p2_bcp", [128, 512], F32)
        wtp = self.psum(stk, "p2_wtp", [128, 2048], BF16)

        s_w = self.ctr("p2_w")
        s_br = self.ctr("p2_br")
        s_bm = self.ctr("p2_bm")
        s_bv = self.ctr("p2_bv")
        s_xg = self.ctr("p2_xg")
        s_km = self.ctr("p2_km")
        s_ke = self.ctr("p2_ke")
        s_vm = self.ctr("p2_vm")
        s_ve = self.ctr("p2_ve")
        s_stk = self.ctr("p2_stk")
        s_stv = self.ctr("p2_stv")
        s_xf = self.ctr("p2_xf")
        s_gt = self.ctr("p2_gt")
        s_gts = self.ctr("p2_gts")

        wk_sy, wk_ac, wk_pe, wk_ve, w2_ev = self.wbuild(
            "w2", self.w_qkv[D:2 * D], wkT_all, fstg, bstg, wtp[:])
        wv_sy, wv_ac, wv_pe, wv_ve, w2_ev = self.wbuild(
            "w2", self.w_qkv[2 * D:3 * D], wvT_all, fstg, bstg, wtp[:], base=32)

        def sy(sy):
            s_br.inc(sy.dma_start(out=bvrow[:], in_=self.b_qkv[None, 2 * D:3 * D]), 16)
            wk_sy(sy)
            wv_sy(sy)
            HI = INTER // 2
            for g in range(NTG):
                if g >= 2:
                    s_km.wait(sy, (g - 1) * 16)
                    s_vm.wait(sy, (g - 1) * 16)
                s_xg.inc(sy.dma_start(
                    out=xng[g % 2][:],
                    in_=self.xn_scr[g].rearrange("p dc t -> p (dc t)")), 16)
                if g >= 1:
                    if g == 1:
                        self.pc["g"].wait(sy)
                    if g == 4:
                        self.pc["u"].wait(sy)
                    for k in range((g - 1) * 9, min(g * 9, 65)):
                        self._wtr(sy, k, wstg, s_gt, s_gts, HI)
            for k in range(63, 65):
                self._wtr(sy, k, wstg, s_gt, s_gts, HI)

        def pe(t):
            wk_pe(t)
            wv_pe(t)
            s_br.wait(t)
            for b in range(4):
                if b >= 1:
                    s_bv.wait(t, b)
                s_bm.inc(t.matmul(bcp[:], self.ones_col[:],
                                  bvrow[0:1, b * 512:(b + 1) * 512],
                                  start=True, stop=True))
            w2_ev.wait(t, 64)
            for g in range(NTG):
                s_xg.wait(t, (g + 1) * 16)
                for ec in range(16):
                    k = g * 16 + ec
                    if k >= 3:
                        s_ke.wait(t, k - 2)
                    for dc in range(16):
                        ins = t.matmul(
                            kps[k % 3][:],
                            wkT_all[:, dc * 2048 + ec * 128:dc * 2048 + (ec + 1) * 128],
                            xng[g % 2][:, dc * 512:(dc + 1) * 512],
                            start=(dc == 0), stop=(dc == 15))
                    s_km.inc(ins)
                for ts in range(4):
                    for eb in range(4):
                        k = g * 16 + ts * 4 + eb
                        if k >= 2:
                            s_ve.wait(t, k - 1)
                        for dc in range(16):
                            ins = t.matmul(
                                vps[k % 2][:],
                                xng[g % 2][:, dc * 512 + ts * 128:dc * 512 + (ts + 1) * 128],
                                wvT_all[:, dc * 2048 + eb * 512:dc * 2048 + (eb + 1) * 512],
                                start=(dc == 0), stop=(dc == 15))
                        s_vm.inc(ins)

        def ac(a):
            wk_ac(a)
            wv_ac(a)
            for g in range(NTG):
                for ec in range(16):
                    k = g * 16 + ec
                    s_km.wait(a, k + 1)
                    if k >= 2:
                        s_stk.wait(a, (k - 1) * 16)
                    s_ke.inc(a.activation(kev[k % 2][:], kps[k % 3][:], AF.Identity,
                                          bias=self.bq_cols[:, 16 + ec:17 + ec], scale=1.0))

        def ve(v):
            wk_ve(v)
            wv_ve(v)
            for b in range(4):
                s_bm.wait(v, b + 1)
                s_bv.inc(v.tensor_copy(bv_bc[:, b * 512:(b + 1) * 512], bcp[:]))
            for g in range(NTG):
                for ts in range(4):
                    for eb in range(4):
                        k = g * 16 + ts * 4 + eb
                        s_vm.wait(v, k + 1)
                        if k >= 2:
                            s_stv.wait(v, (k - 1) * 64)
                        s_ve.inc(v.scalar_tensor_tensor(
                            vev[k % 2][:], vps[k % 2][:], 1.0,
                            bv_bc[:, eb * 512:(eb + 1) * 512], ALU.bypass, ALU.add))

        def gp(g):
            for gi in range(NTG):
                s = gi // 4
                for ec in range(16):
                    k = gi * 16 + ec
                    s_ke.wait(g, k + 1)
                    s_stk.inc(g.dma_start(
                        out=self.k_scr[ec][:, gi * 512:(gi + 1) * 512],
                        in_=kev[k % 2][:]), 16)
                for ts in range(4):
                    tt_in_seq = (gi % 4) * 4 + ts
                    for eb in range(4):
                        k = gi * 16 + ts * 4 + eb
                        s_ve.wait(g, k + 1)
                        for hh in range(4):
                            h = eb * 4 + hh
                            s_stv.inc(g.dma_start(
                                out=self.v_scr[s, h, tt_in_seq],
                                in_=vev[k % 2][:, hh * 128:(hh + 1) * 128]), 16)
            s_stk.wait(g)
            s_stv.wait(g)
            s_gts.wait(g)

        self.blk.sync(sy)
        self.blk.tensor(pe)
        self.blk.scalar(ac)
        self.blk.vector(ve)
        self.blk.gpsimd(gp)
        self.emit_barrier()
        stk.close()

    @staticmethod
    def p4_batches(qg):
        out = []
        for pqs in range(4):
            n = qg * 4 + pqs + 1
            kt0 = 0
            while n > 0:
                nb = min(8, n)
                out.append((pqs, kt0, nb))
                kt0 += nb
                n -= nb
        return out

    def ph_p34(self):
        stk = ExitStack()
        kh = [self.sb(stk, f"p4_k{i}", [128, SEQ], BF16) for i in range(2)]
        vh = [self.sb(stk, f"p4_v{i}", [128, SEQ], BF16) for i in range(2)]
        qh = [self.sb(stk, f"p4_q{i}", [128, 512], BF16) for i in range(2)]
        pt = self.sb(stk, "p4_pt", [128, 16 * 512], BF16)
        pb = [self.sb(stk, f"p4_pb{i}", [128, SEQ], BF16) for i in range(2)]
        cols = self.sb(stk, "p4_cols", [128, 16], F32)
        rrow = self.sb(stk, "p4_rrow", [1, 512], F32)
        rbc = self.sb(stk, "p4_rbc", [128, 512], F32)
        aoe = [self.sb(stk, f"p4_ao{i}", [128, 512], BF16) for i in range(2)]
        xnr = [self.sb(stk, f"p3_xn{dc}", [128, T], BF16) for dc in range(16)]
        gT = [self.sb(stk, f"p3_g{i}", [128, 2048], BF16) for i in range(2)]
        uT = [self.sb(stk, f"p3_u{i}", [128, 2048], BF16) for i in range(2)]
        sil = [self.sb(stk, f"p3_sil{i}", [128, 512], F32) for i in range(2)]
        hbf = [self.sb(stk, f"p3_h{i}", [128, 512], BF16) for i in range(2)]
        dstg = [self.sb(stk, "p34_ds0", [128, 2048], BF16)] * 2
        sps = self.psum(stk, "p4_sps", [128, SEQ], F32)
        tps = self.psum(stk, "p4_tps", [128, 2048], BF16)
        aor = self.psum(stk, "p4_aor", [128, 512], F32)
        gup = self.psum(stk, "p3_gup", [128, 512], F32)
        gps = gup
        ups = gup

        s_kv = self.ctr("p4_kv")
        s_q = self.ctr("p4_q")
        s_sc = self.ctr("p4_sc")
        s_mx = self.ctr("p4_mx")
        s_ex = self.ctr("p4_ex")
        s_rc = self.ctr("p4_rc")
        s_tp = self.ctr("p4_tp")
        s_te = self.ctr("p4_te")
        s_rr = self.ctr("p4_rr")
        s_re = self.ctr("p4_re")
        s_rb = self.ctr("p4_rb")
        s_rbe = self.ctr("p4_rbe")
        s_pv = self.ctr("p4_pv")
        s_ae = self.ctr("p4_ae")
        s_sta = self.ctr("p4_sta")
        s_xnr = self.ctr("p3_xnr")
        s_w3 = self.ctr("p3_w")
        s_gm = self.ctr("p3_gm")
        s_um = self.ctr("p3_um")
        s_sil = self.ctr("p3_sil")
        s_hd = self.ctr("p3_hd")
        s_st3 = self.ctr("p3_st")
        s_dt = self.ctr("p34_dt")
        s_dts = self.ctr("p34_dts")

        SH = [(s, h) for s in range(NSEQ) for h in range(H)]

        att_ev = []
        for j in range(32):
            for qg in range(4):
                jq = j * 4 + qg
                batches = self.p4_batches(qg)
                ntr = {p: sum(nb for q_, _, nb in batches if q_ == p) for p in range(4)}
                if jq == 0:
                    att_ev.append((("sc", 0, 0), 800))
                for qs in range(1, 4):
                    c = qg * 512 + (qs + 1) * 128 + 134 + ntr[qs - 1] * 140
                    att_ev.append((("sc", jq, qs), c))
                att_ev.append((("t1", jq), ntr[3] * 140 + 100))
                att_ev.append((("t2", jq), 700))
                if jq + 1 < 128:
                    att_ev.append((("sc", jq + 1, 0), (jq % 4 == 3) * 0 + ((jq + 1) % 4) * 512 + 262))
                att_ev.append((("t3", jq), 2100))
                att_ev.append((("t4", jq), (4 * qg + 4) * 512 + 200))
        p3_ev = []
        for ic in range(NIC):
            for tg in range(NTG):
                p3_ev.append((("pg", ic, tg), 16 * 512 + 128))
                p3_ev.append((("pu", ic, tg), 16 * 512 + 128))
        tot_att = sum(c for _, c in att_ev)
        tot_p3 = sum(c for _, c in p3_ev)
        ratio = tot_p3 / tot_att
        sched = []
        acc_a = acc_p = 0.0
        pi = 0
        for ev, c in att_ev:
            sched.append(ev)
            acc_a += c
            npump = 0
            while pi < len(p3_ev) and acc_p < acc_a * ratio and npump < 2:
                sched.append(p3_ev[pi][0])
                acc_p += p3_ev[pi][1]
                pi += 1
                npump += 1
        while pi < len(p3_ev):
            sched.append(p3_ev[pi][0])
            pi += 1

        GB = {}
        gb_running = 0
        for jq in range(128):
            batches = self.p4_batches(jq % 4)
            GB[jq] = (gb_running, batches)
            gb_running += len(batches)

        pieces = [(self.dw16, self.dw16T, k) for k in range(NIC)] + \
                 [(self.wo16, self.wo16T, k) for k in range(16)]
        pidx = [0]

        def emit_piece(sy):
            if pidx[0] >= len(pieces):
                return
            src, dstT, k = pieces[pidx[0]]
            i = pidx[0]
            if i == 0:
                self.pc["dw"].wait(sy, (D // 256) * 16)
            if src is self.wo16 and k == 0:
                self.pc["wo"].wait(sy, (D // 512) * 16)
            if i >= 1:
                s_dts.wait(sy, i * 16)
            s_dt.inc(sy.dma_start_transpose(
                out=dstg[0][:], in_=src[:, k * 128:(k + 1) * 128]), 16)
            s_dt.wait(sy, (i + 1) * 16)
            s_dts.inc(sy.dma_start(
                out=dstT[k * 128:(k + 1) * 128, :], in_=dstg[0][:]), 16)
            pidx[0] += 1

        def sy(sy):
            s_kv.inc(sy.dma_start(out=kh[0][:],
                                  in_=self.k_scr[SH[0][1]][:, 0:SEQ]), 16)
            s_kv.inc(sy.dma_start(
                out=vh[0][:].rearrange("k (a c) -> k a c", a=16),
                in_=self.v_scr[0, SH[0][1]].rearrange("a k c -> k a c")), 16)
            s_q.inc(sy.dma_start(out=qh[0][:], in_=self.q_scr[SH[0][1]][:, 0:512]), 16)
            for dc in range(16):
                s_xnr.inc(sy.dma_start(
                    out=xnr[dc][:].rearrange("p (g t) -> p g t", g=NTG),
                    in_=self.xn_scr[:, :, dc, :].rearrange("g p t -> p g t")), 16)
            def load_w(ic):
                for buf, w16T in ((gT, self.g16T), (uT, self.u16T)):
                    s_w3.inc(sy.dma_start(
                        out=buf[ic % 2][:].rearrange("p (dc i) -> p dc i", i=128),
                        in_=w16T.rearrange("(dc p) i -> p dc i", p=128)[
                            :, :, ic * 128:(ic + 1) * 128]), 16)

            for ic0 in (0, 1):
                load_w(ic0)
            for ev in sched:
                if ev[0] == "sc":
                    _, jq, qs = ev
                    if qs != 0:
                        continue
                    j, qg = jq // 4, jq % 4
                    if qg == 0 and j + 1 < 32:
                        jn = j + 1
                        s, h = SH[jn]
                        if jn >= 2:
                            s_pv.wait(sy, (jn - 1) * 4)
                        s_kv.inc(sy.dma_start(
                            out=kh[jn % 2][:],
                            in_=self.k_scr[h][:, s * SEQ:(s + 1) * SEQ]), 16)
                        s_kv.inc(sy.dma_start(
                            out=vh[jn % 2][:].rearrange("k (a c) -> k a c", a=16),
                            in_=self.v_scr[s, h].rearrange("a k c -> k a c")), 16)
                    if jq + 1 < 128:
                        jqn = jq + 1
                        jn2, qgn = jqn // 4, jqn % 4
                        s2, h2 = SH[jn2]
                        if jqn >= 2:
                            s_pv.wait(sy, jqn - 1)
                        s_q.inc(sy.dma_start(
                            out=qh[jqn % 2][:],
                            in_=self.q_scr[h2][:, s2 * SEQ + qgn * 512:
                                               s2 * SEQ + (qgn + 1) * 512]), 16)
                    if jq >= 2 and jq % 2 == 0:
                        emit_piece(sy)
                elif ev[0] == "pu":
                    _, ic, tg = ev
                    if tg == NTG - 1 and ic + 2 < NIC:
                        s_um.wait(sy, (ic + 1) * 8)
                        load_w(ic + 2)


        def sy_tail(sy):
            while pidx[0] < len(pieces):
                emit_piece(sy)
            s_dts.wait(sy)

        def pe(t):
            first_p3 = True
            gb = {}
            for ev in sched:
                if ev[0] == "sc":
                    _, jq, qs = ev
                    j, qg = jq // 4, jq % 4
                    iq = jq * 4 + qs
                    qt = qg * 4 + qs
                    if qs == 0:
                        t.wait_ge(s_kv.sem, (j + 1) * 32)
                        t.wait_ge(s_q.sem, (jq + 1) * 16)
                    soff = (qs % 2) * (qg + 1) * 512 if qg <= 1 else 0
                    if qg <= 1 and qs >= 1:
                        s_ex.wait(t, iq - 1)
                    else:
                        s_ex.wait(t, iq)
                    lhs = qh[jq % 2][:, qs * 128:(qs + 1) * 128]
                    for kc in range(qg):
                        ins = t.matmul(sps[:, soff + kc * 512:soff + (kc + 1) * 512],
                                       lhs,
                                       kh[j % 2][:, kc * 512:(kc + 1) * 512],
                                       start=True, stop=True)
                        if kc == qg - 1:
                            s_sc.inc(ins)
                    bank = sps[:, soff + qg * 512:soff + (qg + 1) * 512]
                    for ks in range(qs + 1):
                        kt = qg * 4 + ks
                        ins = t.matmul(bank[:, ks * 128:(ks + 1) * 128], lhs,
                                       kh[j % 2][:, kt * 128:(kt + 1) * 128],
                                       start=(ks == 0), stop=False)
                        if qg == 0 and ks == qs:
                            s_sc.inc(ins)
                    s_sc.inc(t.matmul(bank[:, qs * 128:(qs + 1) * 128],
                                      self.ident_bf[:], self.maskU[:],
                                      start=False, stop=True))
                    if qs >= 1:
                        s_ex.wait(t, iq)
                        gb0, batches = GB[jq]
                        pos = gb.get(jq, 0)
                        while pos < len(batches) and batches[pos][0] == qs - 1:
                            pqs, kt0, nb = batches[pos]
                            half = ((gb0 + pos) % 2) * 1024
                            if gb0 + pos >= 1:
                                s_te.wait(t, gb0 + pos - 1)
                            for u in range(nb):
                                ins = t.transpose(
                                    tps[:, half + u * 128:half + (u + 1) * 128],
                                    pb[pqs % 2][:, (kt0 + u) * 128:(kt0 + u + 1) * 128],
                                    self.ident_bf[:])
                            s_tp.inc(ins)
                            pos += 1
                        gb[jq] = pos
                elif ev[0] == "t1":
                    jq = ev[1]
                    s_ex.wait(t, jq * 4 + 4)
                    gb0, batches = GB[jq]
                    pos = gb.get(jq, 0)
                    while pos < len(batches):
                        pqs, kt0, nb = batches[pos]
                        half = ((gb0 + pos) % 2) * 1024
                        if gb0 + pos >= 1:
                            s_te.wait(t, gb0 + pos - 1)
                        for u in range(nb):
                            ins = t.transpose(
                                tps[:, half + u * 128:half + (u + 1) * 128],
                                pb[pqs % 2][:, (kt0 + u) * 128:(kt0 + u + 1) * 128],
                                self.ident_bf[:])
                        s_tp.inc(ins)
                        pos += 1
                    gb[jq] = pos
                elif ev[0] == "t2":
                    jq = ev[1]
                    s_ae.wait(t, jq)
                    s_rc.wait(t, 4 * jq + 4)
                    par = 8 + (jq % 2) * 4
                    for qs in range(4):
                        ins = t.transpose(aor[0:1, qs * 128:(qs + 1) * 128],
                                          cols[:, par + qs:par + qs + 1],
                                          self.ident_f[:])
                    s_rr.inc(ins)
                elif ev[0] == "t3":
                    jq = ev[1]
                    s_re.wait(t, jq + 1)
                    s_rb.inc(t.matmul(aor[:], self.ones_col[:], rrow[0:1, :],
                                      start=True, stop=True))
                elif ev[0] == "t4":
                    jq = ev[1]
                    j, qg = jq // 4, jq % 4
                    gb0, batches = GB[jq]
                    s_rbe.wait(t, jq + 1)
                    s_te.wait(t, gb0 + len(batches))
                    for kt in range(4 * qg + 4):
                        ki = kt - 4 * qg
                        lo = max(0, ki) * 128
                        ins = t.matmul(aor[:, lo:512],
                                       vh[j % 2][:, kt * 128:(kt + 1) * 128],
                                       pt[:, kt * 512 + lo:(kt + 1) * 512],
                                       start=(kt == 0), stop=(kt == 4 * qg + 3))
                    s_pv.inc(ins)
                elif ev[0] == "pg":
                    _, ic, tg = ev
                    k = ic * NTG + tg
                    if first_p3:
                        s_xnr.wait(t)
                        first_p3 = False
                    if tg == 0:
                        s_w3.wait(t, (ic + 1) * 2 * 16)
                    s_hd.wait(t, k)
                    for dc in range(16):
                        ins = t.matmul(gps[:], gT[ic % 2][:, dc * 128:(dc + 1) * 128],
                                       xnr[dc][:, tg * 512:(tg + 1) * 512],
                                       start=(dc == 0), stop=(dc == 15))
                    s_gm.inc(ins)
                else:
                    _, ic, tg = ev
                    k = ic * NTG + tg
                    s_sil.wait(t, k + 1)
                    for dc in range(16):
                        ins = t.matmul(ups[:], uT[ic % 2][:, dc * 128:(dc + 1) * 128],
                                       xnr[dc][:, tg * 512:(tg + 1) * 512],
                                       start=(dc == 0), stop=(dc == 15))
                    s_um.inc(ins)

        def ac(a):
            for ev in sched:
                if ev[0] == "sc":
                    _, jq, qs = ev
                    qg = jq % 4
                    iq = jq * 4 + qs
                    L = (qg * 4 + qs + 1) * 128
                    soff = (qs % 2) * (qg + 1) * 512 if qg <= 1 else 0
                    s_mx.wait(a, iq + 1)
                    s_ex.inc(a.activation(pb[qs % 2][:, 0:L], sps[:, soff:soff + L],
                                          AF.Exp, bias=cols[:, qs % 2:qs % 2 + 1],
                                          scale=1.0,
                                          accum_out=cols[:, 4 + qs:5 + qs]))
                elif ev[0] == "pg":
                    _, ic, tg = ev
                    k = ic * NTG + tg
                    s_gm.wait(a, k + 1)
                    if k >= 2:
                        s_hd.wait(a, k - 1)
                    s_sil.inc(a.activation(sil[k % 2][:], gps[:], AF.Silu,
                                           bias=0.0, scale=self.gs_cols[:, ic:ic + 1]))

        def ve(v):
            gb = {}
            for ev in sched:
                if ev[0] == "sc":
                    _, jq, qs = ev
                    qg = jq % 4
                    iq = jq * 4 + qs
                    L = (qg * 4 + qs + 1) * 128
                    par = 8 + (jq % 2) * 4
                    soff = (qs % 2) * (qg + 1) * 512 if qg <= 1 else 0
                    if qg >= 1:
                        s_sc.wait(v, 2 * iq + 1)
                        v.reduce_max(cols[:, 2:3], sps[:, soff:soff + qg * 512],
                                     axis=AX.X, negate=True)
                        s_sc.wait(v, 2 * iq + 2)
                        self.vs(v, v.reduce_max(cols[:, 3:4],
                                                sps[:, soff + qg * 512:soff + L],
                                                axis=AX.X, negate=True))
                        s_mx.inc(v.tensor_tensor(cols[:, qs % 2:qs % 2 + 1],
                                                 cols[:, 2:3],
                                                 cols[:, 3:4], ALU.min))
                    else:
                        s_sc.wait(v, 2 * iq + 2)
                        s_mx.inc(v.reduce_max(cols[:, qs % 2:qs % 2 + 1],
                                              sps[:, soff:soff + L],
                                              axis=AX.X, negate=True))
                    s_ex.wait(v, iq + 1)
                    s_rc.inc(v.reciprocal(cols[:, par + qs:par + qs + 1],
                                          cols[:, 4 + qs:5 + qs]))
                    if qs >= 1:
                        gb0, batches = GB[jq]
                        pos = gb.get(jq, 0)
                        while pos < len(batches) and batches[pos][0] == qs - 1:
                            pqs, kt0, nb = batches[pos]
                            s_tp.wait(v, gb0 + pos + 1)
                            half = ((gb0 + pos) % 2) * 1024
                            dst = pt[:].rearrange("p (kt q) -> p kt q", kt=16)[
                                :, kt0:kt0 + nb, pqs * 128:(pqs + 1) * 128]
                            src = tps[:, half:half + nb * 128].rearrange(
                                "p (a b) -> p a b", b=128)
                            s_te.inc(v.tensor_copy(dst, src))
                            pos += 1
                        gb[jq] = pos
                elif ev[0] == "t1":
                    jq = ev[1]
                    gb0, batches = GB[jq]
                    pos = gb.get(jq, 0)
                    while pos < len(batches):
                        pqs, kt0, nb = batches[pos]
                        s_tp.wait(v, gb0 + pos + 1)
                        half = ((gb0 + pos) % 2) * 1024
                        dst = pt[:].rearrange("p (kt q) -> p kt q", kt=16)[
                            :, kt0:kt0 + nb, pqs * 128:(pqs + 1) * 128]
                        src = tps[:, half:half + nb * 128].rearrange(
                            "p (a b) -> p a b", b=128)
                        s_te.inc(v.tensor_copy(dst, src))
                        pos += 1
                    gb[jq] = pos
                elif ev[0] == "t2":
                    jq = ev[1]
                    s_rr.wait(v, jq + 1)
                    s_re.inc(v.tensor_copy(rrow[0:1, :], aor[0:1, 0:512]))
                elif ev[0] == "t3":
                    jq = ev[1]
                    s_rb.wait(v, jq + 1)
                    s_rbe.inc(v.tensor_copy(rbc[:], aor[:]))
                    s_rbe.wait(v)
                elif ev[0] == "t4":
                    jq = ev[1]
                    s_pv.wait(v, jq + 1)
                    if jq >= 2:
                        s_sta.wait(v, (jq - 1) * 16)
                    s_ae.inc(v.scalar_tensor_tensor(aoe[jq % 2][:], aor[:], 1.0,
                                                    rbc[:], ALU.bypass, ALU.mult))
                elif ev[0] == "pu":
                    _, ic, tg = ev
                    k = ic * NTG + tg
                    s_um.wait(v, k + 1)
                    s_sil.wait(v, k + 1)
                    if k >= 2:
                        s_st3.wait(v, (k - 1) * 16)
                    s_hd.inc(v.scalar_tensor_tensor(
                        hbf[k % 2][:], ups[:], self.us_cols[:, ic:ic + 1],
                        sil[k % 2][:], ALU.mult, ALU.mult))

        def gp(g):
            for src_t, dst, nrows, width, ctr, step in (
                    (self.down_w, self.dw16, D, INTER, self.pc["dw"], 256),
                    (self.w_out, self.wo16, D, D, self.pc["wo"], 512)):
                for r0 in range(0, nrows, step):
                    r1 = min(r0 + step, nrows)
                    ctr.inc(g.dma_start(out=dst[r0:r1, :], in_=src_t[r0:r1, :]), 16)
            for ev in sched:
                if ev[0] == "t4":
                    jq = ev[1]
                    if jq >= 1:
                        s_ae.wait(g, jq)
                        s_sta.inc(g.dma_start(out=self._ao_dst(jq - 1, SH),
                                              in_=aoe[(jq - 1) % 2][:]), 16)
                elif ev[0] == "pu":
                    _, ic, tg = ev
                    k = ic * NTG + tg
                    s_hd.wait(g, k + 1)
                    s_st3.inc(g.dma_start(
                        out=self.hdn_scr[ic][:, tg * 512:(tg + 1) * 512],
                        in_=hbf[k % 2][:]), 16)
            s_ae.wait(g, 128)
            s_sta.inc(g.dma_start(out=self._ao_dst(127, SH), in_=aoe[127 % 2][:]), 16)
            s_sta.wait(g)
            s_st3.wait(g)

        self.blk.sync(sy)
        self.blk.sync(sy_tail)
        self.blk.tensor(pe)
        self.blk.scalar(ac)
        self.blk.vector(ve)
        self.blk.gpsimd(gp)
        self.emit_barrier()
        stk.close()

    def _ao_dst(self, jq, SH):
        j, qg = jq // 4, jq % 4
        s, h = SH[j]
        return self.ao_scr[h][:, s * SEQ + qg * 512:s * SEQ + (qg + 1) * 512]

    def ph_p5(self):
        stk = ExitStack()
        dwT = [self.sb(stk, f"p5_dw{ic}", [128, 1024], BF16) for ic in range(NIC)]
        slab = [self.sb(stk, f"p5_sl{i}", [128, NIC * 128], BF16) for i in range(2)]
        fev = [self.sb(stk, f"p5_f{i}", [128, 512], BF16) for i in range(2)]
        dps = [self.psum(stk, f"p5_dp{i}", [128, 512], F32) for i in range(2)]

        s_w = self.ctr("p5_w")
        s_sl = self.ctr("p5_sl")
        s_dm = self.ctr("p5_dm")
        s_de = self.ctr("p5_de")
        s_st = self.ctr("p5_st")
        s_hf = self.ctr("p5_hf")

        def sy(sy):
            for half in range(2):
                if half == 1:
                    s_dm.wait(sy, 64)
                for ic in range(NIC):
                    s_w.inc(sy.dma_start(
                        out=dwT[ic][:],
                        in_=self.dw16T[ic * 128:(ic + 1) * 128,
                                       half * 1024:(half + 1) * 1024]), 16)
                for tt in range(NTT):
                    i = half * NTT + tt
                    if i >= 2:
                        s_dm.wait(sy, (i - 1) * 2)
                    for b in range(4):
                        nic = 11
                        s_sl.inc(sy.dma_start(
                            out=slab[i % 2][:, b * nic * 128:(b + 1) * nic * 128].rearrange(
                                "p (a t) -> p a t", a=nic),
                            in_=self.hdn_scr[b * nic:(b + 1) * nic, :,
                                             tt * 128:(tt + 1) * 128].rearrange(
                                "a p t -> p a t")), 16)

        def pe(t):
            for half in range(2):
                s_w.wait(t, (half + 1) * NIC * 16)
                for tt in range(NTT):
                    i = half * NTT + tt
                    s_sl.wait(t, (i + 1) * 64)
                    for hb in range(2):
                        k = i * 2 + hb
                        if k >= 2:
                            s_de.wait(t, k - 1)
                        for ic in range(NIC):
                            ins = t.matmul(dps[k % 2][:],
                                           slab[i % 2][:, ic * 128:(ic + 1) * 128],
                                           dwT[ic][:, hb * 512:(hb + 1) * 512],
                                           start=(ic == 0), stop=(ic == NIC - 1))
                        s_dm.inc(ins)

        def ve(v):
            for half in range(2):
                for tt in range(NTT):
                    for hb in range(2):
                        k = (half * NTT + tt) * 2 + hb
                        s_dm.wait(v, k + 1)
                        if k >= 2:
                            s_st.wait(v, (k - 1) * 16)
                        s_de.inc(v.tensor_copy(fev[k % 2][:], dps[k % 2][:]))

        def gp(g):
            for half in range(2):
                for tt in range(NTT):
                    for hb in range(2):
                        k = (half * NTT + tt) * 2 + hb
                        s_de.wait(g, k + 1)
                        s_st.inc(g.dma_start(
                            out=self.ffn_scr[tt][:, half * 1024 + hb * 512:
                                                half * 1024 + (hb + 1) * 512],
                            in_=fev[k % 2][:]), 16)
            s_st.wait(g)

        self.blk.sync(sy)
        self.blk.tensor(pe)
        self.blk.vector(ve)
        self.blk.gpsimd(gp)
        self.blk.scalar(lambda a: None)
        self.emit_barrier()
        stk.close()

    def ph_p6(self):
        stk = ExitStack()
        woT = [self.sb(stk, f"p6_wo{dc}", [128, D], BF16) for dc in range(16)]
        aog = [[self.sb(stk, f"p6_a{i}_{hh}", [128, 512], BF16) for hh in range(16)]
               for i in range(2)]
        xb = [self.sb(stk, f"p6_x{i}", [128, D], F32) for i in range(2)]
        fb = [self.sb(stk, f"p6_f{i}", [128, D], BF16) for i in range(2)]
        yb = [self.sb(stk, f"p6_y{i}", [128, D], F32) for i in range(2)]
        acc1 = [self.sb(stk, f"p6_c1{i}", [128, D], F32) for i in range(2)]
        acc2 = self.sb(stk, "p6_c2", [128, D], F32)
        borow = self.sb(stk, "p6_bor", [1, D], F32)
        dsrow = self.sb(stk, "p6_dsr", [1, D], F32)
        bo_bc = self.sb(stk, "p6_bobc", [128, D], F32)
        ds_bc = self.sb(stk, "p6_dsbc", [128, D], F32)
        prps = [self.psum(stk, f"p6_pp{i}", [128, D], F32) for i in range(2)]

        s_r = self.ctr("p6_r")
        s_bm = self.ctr("p6_bm")
        s_be = self.ctr("p6_be")
        s_wo = self.ctr("p6_wo")
        s_ao = self.ctr("p6_ao")
        s_x = self.ctr("p6_x")
        s_f = self.ctr("p6_f")
        s_pm = self.ctr("p6_pm")
        s_ye = self.ctr("p6_ye")
        s_st = self.ctr("p6_st")

        def sy(sy):
            s_r.inc(sy.dma_start(out=borow[:], in_=self.b_out[None, :]), 16)
            s_r.inc(sy.dma_start(out=dsrow[:], in_=self.down_s[None, :]), 16)
            for dc in range(16):
                s_wo.inc(sy.dma_start(
                    out=woT[dc][:],
                    in_=self.wo16T[dc * 128:(dc + 1) * 128, :]), 16)
            for g in range(NTG):
                if g >= 2:
                    s_pm.wait(sy, (g - 1) * 4)
                for hh in range(16):
                    s_ao.inc(sy.dma_start(
                        out=aog[g % 2][hh][:],
                        in_=self.ao_scr[hh][:, g * 512:(g + 1) * 512]), 16)
                for ts in range(4):
                    tt = g * 4 + ts
                    s = tt // 16
                    r = (tt % 16) * 128
                    if tt >= 2:
                        s_ye.wait(sy, tt - 1)
                    s_x.inc(sy.dma_start(out=xb[tt % 2][:],
                                         in_=self.x[s, r:r + 128, :]), 16)
                    s_f.inc(sy.dma_start(out=fb[tt % 2][:], in_=self.ffn_scr[tt]), 16)

        def pe(t):
            s_r.wait(t)
            for b in range(8):
                if b >= 1:
                    s_be.wait(t, b)
                row = borow if b < 4 else dsrow
                s_bm.inc(t.matmul(prps[0][:, 0:512], self.ones_col[:],
                                  row[0:1, (b % 4) * 512:(b % 4 + 1) * 512],
                                  start=True, stop=True))
            s_wo.wait(t)
            s_be.wait(t, 8)
            for tt in range(NTT):
                g, ts = tt // 4, tt % 4
                s_ao.wait(t, (g + 1) * 256)
                if tt >= 2:
                    s_ye.wait(t, tt - 1)
                for eb in range(4):
                    for dc in range(16):
                        ins = t.matmul(prps[tt % 2][:, eb * 512:(eb + 1) * 512],
                                       aog[g % 2][dc][:, ts * 128:(ts + 1) * 128],
                                       woT[dc][:, eb * 512:(eb + 1) * 512],
                                       start=(dc == 0), stop=(dc == 15))
                s_pm.inc(ins)

        def ve(v):
            for b in range(8):
                s_bm.wait(v, b + 1)
                dst = bo_bc if b < 4 else ds_bc
                s_be.inc(v.tensor_copy(dst[:, (b % 4) * 512:(b % 4 + 1) * 512],
                                       prps[0][:, 0:512]))
            for tt in range(NTT):
                s_f.wait(v, (tt + 1) * 16)
                s_x.wait(v, (tt + 1) * 16)
                v.scalar_tensor_tensor(acc1[tt % 2][:], fb[tt % 2][:], 1.0,
                                       ds_bc[:], ALU.bypass, ALU.mult)
                self.vs(v, v.scalar_tensor_tensor(acc2[:], xb[tt % 2][:], 1.0,
                                                  bo_bc[:], ALU.bypass, ALU.add))
                self.vs(v, v.tensor_tensor(acc1[tt % 2][:], acc1[tt % 2][:],
                                           acc2[:], ALU.add))
                s_pm.wait(v, tt + 1)
                if tt >= 2:
                    s_st.wait(v, (tt - 1) * 16)
                s_ye.inc(v.scalar_tensor_tensor(yb[tt % 2][:], prps[tt % 2][:], 1.0,
                                                acc1[tt % 2][:], ALU.bypass, ALU.add))

        def gp(g):
            for tt in range(NTT):
                s = tt // 16
                r = (tt % 16) * 128
                s_ye.wait(g, tt + 1)
                s_st.inc(g.dma_start(out=self.y[s, r:r + 128, :], in_=yb[tt % 2][:]), 16)
            s_st.wait(g)

        self.blk.sync(sy)
        self.blk.tensor(pe)
        self.blk.vector(ve)
        self.blk.gpsimd(gp)
        self.blk.scalar(lambda a: None)
        stk.close()


def build_nc(dbg=()):
    return B(dbg=dbg).build()


_NC_CACHE = {}


def kernel(**inputs):
    key = "main"
    if key not in _NC_CACHE:
        _NC_CACHE[key] = build_nc()
    nc = _NC_CACHE[key]
    x = np.ascontiguousarray(inputs["L0_attn_x"])
    base = {
        "w_qkv": np.ascontiguousarray(inputs["L0_attn_w_qkv"]),
        "b_qkv": np.ascontiguousarray(inputs["L0_attn_b_qkv"]),
        "w_out": np.ascontiguousarray(inputs["L0_attn_w_out"]),
        "b_out": np.ascontiguousarray(inputs["L0_attn_b_out"]),
        "gate_w": np.ascontiguousarray(inputs["L0_ffn_gate_w"]),
        "up_w": np.ascontiguousarray(inputs["L0_ffn_up_w"]),
        "down_w": np.ascontiguousarray(inputs["L0_ffn_down_w"]),
        "gate_s": np.ascontiguousarray(inputs["L0_ffn_gate_s"]),
        "up_s": np.ascontiguousarray(inputs["L0_ffn_up_s"]),
        "down_s": np.ascontiguousarray(inputs["L0_ffn_down_s"]),
    }
    in_maps = [dict(base, x=np.ascontiguousarray(x[2 * i:2 * i + 2]))
               for i in range(8)]
    res = run_bass_kernel_spmd(nc, in_maps, core_ids=list(range(8)))
    return np.concatenate([r["y"] for r in res.results], axis=0)


# version 26
# speedup vs baseline: 1.0694x; 1.0013x over previous
import math
from contextlib import ExitStack

import numpy as np

import concourse.bass as bass
import concourse.mybir as mybir
from concourse.bass_utils import run_bass_kernel_spmd

F32 = mybir.dt.float32
BF16 = mybir.dt.bfloat16
AF = mybir.ActivationFunctionType
ALU = mybir.AluOpType
AX = mybir.AxisListType

D = 2048
H = 16
HD = 128
INTER = 5632
SEQ = 2048
NSEQ = 2
T = NSEQ * SEQ
NTT = T // 128
NTG = T // 512
NIC = INTER // 128
NIG2 = INTER // 256
RS = 1.0 / math.sqrt(HD)
EPS = 1e-5
NEGINF = -1e9


class Ctr:

    def __init__(self, nc, name):
        self.sem = nc.alloc_semaphore(name)
        self.n = 0

    def inc(self, inst, k=1):
        inst.then_inc(self.sem, k)
        self.n += k
        return self.n

    def wait(self, eng, n=None):
        eng.wait_ge(self.sem, self.n if n is None else n)


class B:
    def __init__(self, dbg=()):
        self.nc = bass.Bass("TRN2", target_bir_lowering=False)
        self.dbg = set(dbg)
        self.ctrs = {}
        self.stack = ExitStack()
        self.bar_ctr = Ctr(self.nc, "bar")
        self.bar_round = {}
        nc = self.nc

        self.dbg_pairs = []

        def dram(name, shape, dtype, kind="Internal"):
            ap = nc.dram_tensor(name, shape, dtype, kind=kind).ap()
            if name in self.dbg:
                dbg_ap = nc.dram_tensor(name + "__dbg", shape, F32,
                                        kind="ExternalOutput").ap()
                self.dbg_pairs.append((ap, dbg_ap))
            return ap

        self.x = nc.dram_tensor("x", [NSEQ, SEQ, D], F32, kind="ExternalInput").ap()
        self.w_qkv = nc.dram_tensor("w_qkv", [3 * D, D], F32, kind="ExternalInput").ap()
        self.b_qkv = nc.dram_tensor("b_qkv", [3 * D], F32, kind="ExternalInput").ap()
        self.w_out = nc.dram_tensor("w_out", [D, D], F32, kind="ExternalInput").ap()
        self.b_out = nc.dram_tensor("b_out", [D], F32, kind="ExternalInput").ap()
        self.gate_w = nc.dram_tensor("gate_w", [INTER, D], F32, kind="ExternalInput").ap()
        self.up_w = nc.dram_tensor("up_w", [INTER, D], F32, kind="ExternalInput").ap()
        self.down_w = nc.dram_tensor("down_w", [D, INTER], F32, kind="ExternalInput").ap()
        self.gate_s = nc.dram_tensor("gate_s", [INTER], F32, kind="ExternalInput").ap()
        self.up_s = nc.dram_tensor("up_s", [INTER], F32, kind="ExternalInput").ap()
        self.down_s = nc.dram_tensor("down_s", [D], F32, kind="ExternalInput").ap()
        self.y = nc.dram_tensor("y", [NSEQ, SEQ, D], F32, kind="ExternalOutput").ap()

        self.wq16 = dram("wq16", [D, D], BF16)
        self.wk16 = dram("wk16", [D, D], BF16)
        self.wv16 = dram("wv16", [D, D], BF16)
        self.g16 = dram("g16", [INTER, D], BF16)
        self.u16 = dram("u16", [INTER, D], BF16)
        self.dw16 = dram("dw16", [D, INTER], BF16)
        self.g16T = dram("g16T", [D, INTER], BF16)
        self.u16T = dram("u16T", [D, INTER], BF16)
        self.dw16T = dram("dw16T", [INTER, D], BF16)
        self.wo16 = dram("wo16", [D, D], BF16)
        self.wo16T = dram("wo16T", [D, D], BF16)

        self.xn_scr = dram("xn_scr", [NTG, 128, 16, 512], BF16)
        self.q_scr = dram("q_scr", [H, 128, T], BF16)
        self.k_scr = dram("k_scr", [H, 128, T], BF16)
        self.v_scr = dram("v_scr", [NSEQ, H, 128, 16, 128], BF16)
        self.hdn_scr = dram("hdn_scr", [NIC, 128, T], BF16)
        self.ao_scr = dram("ao_scr", [H, 128, T], BF16)
        self.ffn_scr = dram("ffn_scr", [NTT, 128, D], BF16)

    def ctr(self, name):
        if name not in self.ctrs:
            self.ctrs[name] = Ctr(self.nc, name)
        return self.ctrs[name]

    def sb(self, stk, name, shape, dtype):
        return stk.enter_context(self.nc.sbuf_tensor(name, shape, dtype))

    def psum(self, stk, name, shape, dtype=F32):
        return stk.enter_context(self.nc.psum_tensor(name, shape, dtype))

    def vs(self, eng, inst):
        c = self.ctr("self_" + str(eng.engine))
        c.inc(inst)
        c.wait(eng)

    def _wtr(self, sy, k, wstg, s_gt, s_gts, HI):
        if k < 64:
            if k >= 2:
                s_gts.wait(sy, (k - 1) * 16)
            w16 = self.g16 if k < 32 else self.u16
            dc, piece = (k % 32) // 2, k % 2
            s_gt.inc(sy.dma_start_transpose(
                out=wstg[k % 2][:],
                in_=w16[piece * HI:(piece + 1) * HI,
                        dc * 128:(dc + 1) * 128]), 16)
        if k >= 1:
            ks = k - 1
            w16T = self.g16T if ks < 32 else self.u16T
            dc, piece = (ks % 32) // 2, ks % 2
            s_gt.wait(sy, (ks + 1) * 16)
            s_gts.inc(sy.dma_start(
                out=w16T[dc * 128:(dc + 1) * 128,
                         piece * HI:(piece + 1) * HI],
                in_=wstg[ks % 2][:]), 16)

    def _wtr_end(self):
        pass

    def wbuild(self, tag, src_rows, wT_all, fstg, bstg, wtp, base=0):
        ld = self.ctr(tag + "_ld")
        ca = self.ctr(tag + "_ca")
        tr = self.ctr(tag + "_tr")
        ev = self.ctr(tag + "_ev")

        def sy_emit(sy):
            for c in range(32):
                ec, half = c // 2, c % 2
                if base + c >= 2:
                    ca.wait(sy, base + c - 1)
                ld.inc(sy.dma_start(
                    out=fstg[c % 2][:],
                    in_=src_rows[ec * 128:(ec + 1) * 128,
                                 half * 1024:(half + 1) * 1024]), 16)

        def ac_emit(a):
            for c in range(32):
                ld.wait(a, (base + c + 1) * 16)
                if base + c >= 2:
                    tr.wait(a, base + c - 1)
                ca.inc(a.activation(bstg[c % 2][:], fstg[c % 2][:], AF.Identity,
                                    bias=0.0, scale=1.0))

        def pe_emit(t):
            for c in range(32):
                ec, half = c // 2, c % 2
                ca.wait(t, base + c + 1)
                if base + c >= 2:
                    ev.wait(t, base + c - 1)
                hoff = (c % 2) * 1024
                for d2 in range(8):
                    ins = t.transpose(wtp[:, hoff + d2 * 128:hoff + (d2 + 1) * 128],
                                      bstg[c % 2][:, d2 * 128:(d2 + 1) * 128],
                                      self.ident_bf[:])
                tr.inc(ins)

        def ve_emit(v):
            for c in range(32):
                ec, half = c // 2, c % 2
                tr.wait(v, base + c + 1)
                hoff = (c % 2) * 1024
                dst = wT_all[:].rearrange("p (dc e) -> p dc e", e=2048)[
                    :, half * 8:half * 8 + 8, ec * 128:(ec + 1) * 128]
                ev.inc(v.tensor_copy(
                    dst, wtp[:, hoff:hoff + 1024].rearrange("p (a b) -> p a b", b=128)))

        return sy_emit, ac_emit, pe_emit, ve_emit, ev

    def bar(self, eng, who):
        r = self.bar_round.get(who, 0) + 1
        self.bar_round[who] = r
        eng.drain()
        eng.nop().then_inc(self.bar_ctr.sem, 1)
        eng.wait_ge(self.bar_ctr.sem, 5 * r)

    def build(self):
        nc = self.nc
        cst = self.stack
        self.ident_bf = self.sb(cst, "ident_bf", [128, 128], BF16)
        self.ident_f = self.sb(cst, "ident_f", [128, 128], F32)
        self.maskU = self.sb(cst, "maskU", [128, 128], BF16)
        self.ones_col = self.sb(cst, "ones_col", [1, 128], F32)
        self.bq_cols = self.sb(cst, "bq_cols", [128, 32], F32)
        self.bqs_cols = self.sb(cst, "bqs_cols", [128, 16], F32)
        self.gs_cols = self.sb(cst, "gs_cols", [128, NIC], F32)
        self.us_cols = self.sb(cst, "us_cols", [128, NIC], F32)

        with nc.Block() as block:
            self.blk = block
            self.ph_consts()
            self.ph_p1()
            self.ph_p2()
            self.ph_p34()
            self.ph_p5()
            self.ph_p6()
            if self.dbg_pairs:
                self.emit_barrier()
                dbgc = self.ctr("dbgc")

                def gp_dbg(g):
                    for ap, dbg_ap in self.dbg_pairs:
                        dbgc.inc(g.dma_start(out=dbg_ap, in_=ap), 16)
                    dbgc.wait(g)
                self.blk.gpsimd(gp_dbg)
        self.stack.close()
        return nc

    def ph_consts(self):
        stk = ExitStack()
        bqr = self.sb(stk, "c_bqr", [128, 128], F32)
        gsr = self.sb(stk, "c_gsr", [128, 128], F32)
        usr = self.sb(stk, "c_usr", [128, 128], F32)
        tp = self.psum(stk, "c_tp", [128, 128], F32)

        c_ld = self.ctr("c_ld")
        c_g = self.ctr("c_g")
        c_pe = self.ctr("c_pe")
        c_v = self.ctr("c_v")
        pc = {w: self.ctr("pc_" + w) for w in ("wq", "wk", "wv", "g", "u", "dw", "wo")}
        self.pc = pc

        def gp(g):
            self.vs(g, g.memset(self.ident_bf[:], 0.0))
            g.affine_select(out=self.ident_bf[:], in_=self.ident_bf[:],
                            compare_op=ALU.not_equal, fill=1.0, base=0,
                            pattern=[[-1, 128]], channel_multiplier=1)
            self.vs(g, g.memset(self.ident_f[:], 0.0))
            g.affine_select(out=self.ident_f[:], in_=self.ident_f[:],
                            compare_op=ALU.not_equal, fill=1.0, base=0,
                            pattern=[[-1, 128]], channel_multiplier=1)
            self.vs(g, g.memset(self.maskU[:], 0.0))
            g.affine_select(out=self.maskU[:], in_=self.maskU[:],
                            compare_op=ALU.is_ge, fill=NEGINF, base=0,
                            pattern=[[-1, 128]], channel_multiplier=1)
            c_g.inc(g.memset(self.ones_col[:], 1.0))
            g.memset(bqr[:], 0.0)
            g.memset(gsr[:], 0.0)
            c_g.inc(g.memset(usr[:], 0.0))

        def sy(sy):
            c_g.wait(sy, 2)
            c_ld.inc(sy.dma_start(out=bqr[0:48, :],
                                  in_=self.b_qkv.rearrange("(c p) -> c p", p=128)), 16)
            c_ld.inc(sy.dma_start(out=gsr[0:44, :],
                                  in_=self.gate_s.rearrange("(c p) -> c p", p=128)), 16)
            c_ld.inc(sy.dma_start(out=usr[0:44, :],
                                  in_=self.up_s.rearrange("(c p) -> c p", p=128)), 16)

        def pe(t):
            c_ld.wait(t)
            c_g.wait(t)
            t2 = t.transpose(tp[:, :], bqr[:, :], self.ident_f[:])
            c_pe.inc(t2)
            c_v.wait(t, 1)
            c_pe.inc(t.transpose(tp[:, :], gsr[:, :], self.ident_f[:]))
            c_v.wait(t, 2)
            c_pe.inc(t.transpose(tp[:, :], usr[:, :], self.ident_f[:]))

        def ve(v):
            c_pe.wait(v, 1)
            ins = v.tensor_copy(self.bq_cols[:, :], tp[:, 0:32])
            c_v.inc(ins)
            c_v.wait(v)
            v.tensor_scalar_mul(self.bqs_cols[:, :], self.bq_cols[:, 0:16], RS)
            c_pe.wait(v, 2)
            c_v.inc(v.tensor_copy(self.gs_cols[:, :], tp[:, 0:44]))
            c_pe.wait(v, 3)
            c_v.inc(v.tensor_copy(self.us_cols[:, :], tp[:, 0:44]))

        def ac(a):
            pass

        self.blk.gpsimd(gp)
        self.blk.sync(sy)
        self.blk.tensor(pe)
        self.blk.vector(ve)
        self.blk.scalar(ac)
        self.emit_barrier()
        stk.close()

    def emit_barrier(self):
        self.blk.gpsimd(lambda g: self.bar(g, "g"))
        self.blk.sync(lambda s: self.bar(s, "s"))
        self.blk.tensor(lambda t: self.bar(t, "t"))
        self.blk.vector(lambda v: self.bar(v, "v"))
        self.blk.scalar(lambda a: self.bar(a, "a"))

    def ph_p1(self):
        stk = ExitStack()
        wqT_all = self.sb(stk, "p1_wqT", [128, 16 * D], BF16)
        fstg = [self.sb(stk, f"p1_wf{i}", [128, 1024], F32) for i in range(2)]
        bstg = [self.sb(stk, f"p1_wb{i}", [128, 1024], BF16) for i in range(2)]
        xb = [self.sb(stk, f"p1_x{i}", [128, D], F32) for i in range(2)]
        xnb = [self.sb(stk, f"p1_xn{i}", [128, D], BF16) for i in range(4)]
        sq = self.sb(stk, "p1_sq", [128, D], F32)
        st = [self.sb(stk, f"p1_st{i}", [128, 8], F32) for i in range(2)]
        xng = [self.sb(stk, f"p1_xng{i}", [128, 16 * 512], BF16) for i in range(2)]
        qev = [self.sb(stk, f"p1_qe{i}", [128, 512], BF16) for i in range(2)]
        trp = [self.psum(stk, f"p1_tr{i}", [128, D], BF16) for i in range(2)]
        qps = [self.psum(stk, f"p1_qp{i}", [128, 512], F32) for i in range(3)]

        s_x = self.ctr("p1_x")
        s_va = self.ctr("p1_va")
        s_ac = self.ctr("p1_ac")
        s_vb = self.ctr("p1_vb")
        s_ad = self.ctr("p1_ad")
        s_vc = self.ctr("p1_vc")
        s_ae = self.ctr("p1_ae")
        s_tr = self.ctr("p1_tr")
        s_ev = self.ctr("p1_ev")
        s_qm = self.ctr("p1_qm")
        s_qe = self.ctr("p1_qe")
        s_stx = self.ctr("p1_stx")
        s_stq = self.ctr("p1_stq")
        wb_sy, wb_ac, wb_pe, wb_ve, w1_ev = self.wbuild(
            "w1", self.w_qkv[0:D], wqT_all, fstg, bstg, trp[0][:])

        def sy(sy):
            for i in range(2):
                s_x.inc(sy.dma_start(out=xb[i % 2][:], in_=self.x[0, i * 128:(i + 1) * 128, :]), 16)
            wb_sy(sy)
            for i in range(2, NTT):
                s, r = i // 16, (i % 16) * 128
                s_ae.wait(sy, i - 1)
                s_x.inc(sy.dma_start(out=xb[i % 2][:], in_=self.x[s, r:r + 128, :]), 16)

        def ve(v):
            wb_ve(v)
            for g in range(NTG):
                for j in range(4):
                    i = g * 4 + j
                    c = st[i % 2]
                    s_x.wait(v, (i + 1) * 16)
                    if i >= 2:
                        s_ae.wait(v, i - 1)
                    self.vs(v, v.reduce_sum(c[:, 0:1], xb[i % 2][:], axis=AX.X))
                    s_va.inc(v.tensor_scalar_mul(c[:, 1:2], c[:, 0:1], -1.0 / D))
                    s_ac.wait(v, i + 1)
                    s_vb.inc(v.tensor_scalar(c[:, 3:4], c[:, 2:3], 1.0 / D, EPS,
                                             ALU.mult, ALU.add))
                    s_ad.wait(v, i + 1)
                    self.vs(v, v.reciprocal(c[:, 5:6], c[:, 4:5]))
                    s_vc.inc(v.tensor_tensor(c[:, 6:7], c[:, 1:2], c[:, 5:6], ALU.mult))
                if g >= 1:
                    qe_group(v, g - 1)
                for j in range(4):
                    i = g * 4 + j
                    off = j * 128
                    if j == 0 and g >= 2:
                        s_qm.wait(v, (g - 1) * 16)
                        s_stx.wait(v, (g - 1) * 16)
                    s_tr.wait(v, i + 1)
                    dst = xng[g % 2][:].rearrange("p (dc t) -> p dc t", dc=16)[:, :, off:off + 128]
                    s_ev.inc(v.tensor_copy(
                        dst, trp[i % 2][:].rearrange("p (dc t) -> p dc t", dc=16)))
            qe_group(v, NTG - 1)

        def qe_group(v, g):
            for ec in range(16):
                k = g * 16 + ec
                s_qm.wait(v, k + 1)
                if k >= 2:
                    s_stq.wait(v, (k - 1) * 16)
                s_qe.inc(v.tensor_scalar(qev[k % 2][:], qps[k % 3][:],
                                         self.bq_cols[:, ec:ec + 1], RS,
                                         ALU.add, ALU.mult))

        def ac(a):
            wb_ac(a)
            for g in range(NTG):
                for j in range(4):
                    i = g * 4 + j
                    c = st[i % 2]
                    s_va.wait(a, i + 1)
                    s_ac.inc(a.activation(sq[:], xb[i % 2][:], AF.Square,
                                          bias=c[:, 1:2], scale=1.0, accum_out=c[:, 2:3]))
                    s_vb.wait(a, i + 1)
                    s_ad.inc(a.activation(c[:, 4:5], c[:, 3:4], AF.Sqrt))
                    s_vc.wait(a, i + 1)
                    if i >= 4:
                        s_tr.wait(a, i - 3)
                    s_ae.inc(a.activation(xnb[i % 4][:], xb[i % 2][:], AF.Identity,
                                          bias=c[:, 6:7], scale=c[:, 5:6]))

        def pe(t):
            wb_pe(t)
            w1_ev.wait(t, 32)
            for g in range(NTG):
                for j in range(4):
                    i = g * 4 + j
                    s_ae.wait(t, i + 1)
                    if i >= 2:
                        s_ev.wait(t, i - 1)
                    for dc in range(16):
                        ins = t.transpose(trp[i % 2][:, dc * 128:(dc + 1) * 128],
                                          xnb[i % 4][:, dc * 128:(dc + 1) * 128],
                                          self.ident_bf[:])
                    s_tr.inc(ins)
                s_ev.wait(t, (g + 1) * 4)
                for ec in range(16):
                    k = g * 16 + ec
                    if k >= 3:
                        s_qe.wait(t, k - 2)
                    for dc in range(16):
                        ins = t.matmul(
                            qps[k % 3][:],
                            wqT_all[:, dc * 2048 + ec * 128:dc * 2048 + (ec + 1) * 128],
                            xng[g % 2][:, dc * 512:(dc + 1) * 512],
                            start=(dc == 0), stop=(dc == 15))
                    s_qm.inc(ins)

        def gp(g):
            for src, dst, ctr in ((self.gate_w, self.g16, self.pc["g"]),
                                  (self.up_w, self.u16, self.pc["u"])):
                for r0 in range(0, INTER, 512):
                    r1 = min(r0 + 512, INTER)
                    ctr.inc(g.dma_start(out=dst[r0:r1, :], in_=src[r0:r1, :]), 16)
            for gi in range(NTG):
                s_ev.wait(g, (gi + 1) * 4)
                s_stx.inc(g.dma_start(
                    out=self.xn_scr[gi].rearrange("p dc t -> p (dc t)"),
                    in_=xng[gi % 2][:]), 16)
                for ec in range(16):
                    k = gi * 16 + ec
                    s_qe.wait(g, k + 1)
                    s_stq.inc(g.dma_start(
                        out=self.q_scr[ec][:, gi * 512:(gi + 1) * 512],
                        in_=qev[k % 2][:]), 16)
            s_stx.wait(g)
            s_stq.wait(g)

        self.blk.sync(sy)
        self.blk.vector(ve)
        self.blk.scalar(ac)
        self.blk.tensor(pe)
        self.blk.gpsimd(gp)
        self.emit_barrier()
        stk.close()

    def ph_p2(self):
        stk = ExitStack()
        wkT_all = self.sb(stk, "p2_wkT", [128, 16 * D], BF16)
        wvT_all = self.sb(stk, "p2_wvT", [128, 16 * D], BF16)
        fstg = [self.sb(stk, f"p2_wf{i}", [128, 1024], F32) for i in range(2)]
        bstg = [self.sb(stk, f"p2_wb{i}", [128, 1024], BF16) for i in range(2)]
        xng = [self.sb(stk, f"p2_xg{i}", [128, 16 * 512], BF16) for i in range(2)]
        kev = [self.sb(stk, f"p2_ke{i}", [128, 512], BF16) for i in range(2)]
        vev = [self.sb(stk, f"p2_ve{i}", [128, 512], BF16) for i in range(2)]
        bvrow = self.sb(stk, "p2_bvr", [1, D], F32)
        bv_bc = self.sb(stk, "p2_bvbc", [128, D], F32)
        wstg = [self.sb(stk, f"p2_ws{i}", [128, INTER // 2], BF16) for i in range(2)]
        kps = [self.psum(stk, f"p2_kp{i}", [128, 512], F32) for i in range(3)]
        vps = [self.psum(stk, f"p2_vp{i}", [128, 512], F32) for i in range(2)]
        bcp = self.psum(stk, "p2_bcp", [128, 512], F32)
        wtp = self.psum(stk, "p2_wtp", [128, 2048], BF16)

        s_w = self.ctr("p2_w")
        s_br = self.ctr("p2_br")
        s_bm = self.ctr("p2_bm")
        s_bv = self.ctr("p2_bv")
        s_xg = self.ctr("p2_xg")
        s_km = self.ctr("p2_km")
        s_ke = self.ctr("p2_ke")
        s_vm = self.ctr("p2_vm")
        s_ve = self.ctr("p2_ve")
        s_stk = self.ctr("p2_stk")
        s_stv = self.ctr("p2_stv")
        s_xf = self.ctr("p2_xf")
        s_gt = self.ctr("p2_gt")
        s_gts = self.ctr("p2_gts")

        wk_sy, wk_ac, wk_pe, wk_ve, w2_ev = self.wbuild(
            "w2", self.w_qkv[D:2 * D], wkT_all, fstg, bstg, wtp[:])
        wv_sy, wv_ac, wv_pe, wv_ve, w2_ev = self.wbuild(
            "w2", self.w_qkv[2 * D:3 * D], wvT_all, fstg, bstg, wtp[:], base=32)

        def sy(sy):
            s_br.inc(sy.dma_start(out=bvrow[:], in_=self.b_qkv[None, 2 * D:3 * D]), 16)
            wk_sy(sy)
            wv_sy(sy)
            HI = INTER // 2
            for g in range(NTG):
                if g >= 2:
                    s_km.wait(sy, (g - 1) * 16)
                    s_vm.wait(sy, (g - 1) * 16)
                s_xg.inc(sy.dma_start(
                    out=xng[g % 2][:],
                    in_=self.xn_scr[g].rearrange("p dc t -> p (dc t)")), 16)
                if g >= 1:
                    if g == 1:
                        self.pc["g"].wait(sy)
                    if g == 4:
                        self.pc["u"].wait(sy)
                    for k in range((g - 1) * 9, min(g * 9, 65)):
                        self._wtr(sy, k, wstg, s_gt, s_gts, HI)
            for k in range(63, 65):
                self._wtr(sy, k, wstg, s_gt, s_gts, HI)

        def pe(t):
            wk_pe(t)
            wv_pe(t)
            s_br.wait(t)
            for b in range(4):
                if b >= 1:
                    s_bv.wait(t, b)
                s_bm.inc(t.matmul(bcp[:], self.ones_col[:],
                                  bvrow[0:1, b * 512:(b + 1) * 512],
                                  start=True, stop=True))
            w2_ev.wait(t, 64)
            for g in range(NTG):
                s_xg.wait(t, (g + 1) * 16)
                for ec in range(16):
                    k = g * 16 + ec
                    if k >= 3:
                        s_ke.wait(t, k - 2)
                    for dc in range(16):
                        ins = t.matmul(
                            kps[k % 3][:],
                            wkT_all[:, dc * 2048 + ec * 128:dc * 2048 + (ec + 1) * 128],
                            xng[g % 2][:, dc * 512:(dc + 1) * 512],
                            start=(dc == 0), stop=(dc == 15))
                    s_km.inc(ins)
                for ts in range(4):
                    for eb in range(4):
                        k = g * 16 + ts * 4 + eb
                        if k >= 2:
                            s_ve.wait(t, k - 1)
                        for dc in range(16):
                            ins = t.matmul(
                                vps[k % 2][:],
                                xng[g % 2][:, dc * 512 + ts * 128:dc * 512 + (ts + 1) * 128],
                                wvT_all[:, dc * 2048 + eb * 512:dc * 2048 + (eb + 1) * 512],
                                start=(dc == 0), stop=(dc == 15))
                        s_vm.inc(ins)

        def ac(a):
            wk_ac(a)
            wv_ac(a)
            for g in range(NTG):
                for ec in range(16):
                    k = g * 16 + ec
                    s_km.wait(a, k + 1)
                    if k >= 2:
                        s_stk.wait(a, (k - 1) * 16)
                    s_ke.inc(a.activation(kev[k % 2][:], kps[k % 3][:], AF.Identity,
                                          bias=self.bq_cols[:, 16 + ec:17 + ec], scale=1.0))

        def ve(v):
            wk_ve(v)
            wv_ve(v)
            for b in range(4):
                s_bm.wait(v, b + 1)
                s_bv.inc(v.tensor_copy(bv_bc[:, b * 512:(b + 1) * 512], bcp[:]))
            for g in range(NTG):
                for ts in range(4):
                    for eb in range(4):
                        k = g * 16 + ts * 4 + eb
                        s_vm.wait(v, k + 1)
                        if k >= 2:
                            s_stv.wait(v, (k - 1) * 64)
                        s_ve.inc(v.scalar_tensor_tensor(
                            vev[k % 2][:], vps[k % 2][:], 1.0,
                            bv_bc[:, eb * 512:(eb + 1) * 512], ALU.bypass, ALU.add))

        def gp(g):
            for gi in range(NTG):
                s = gi // 4
                for ec in range(16):
                    k = gi * 16 + ec
                    s_ke.wait(g, k + 1)
                    s_stk.inc(g.dma_start(
                        out=self.k_scr[ec][:, gi * 512:(gi + 1) * 512],
                        in_=kev[k % 2][:]), 16)
                for ts in range(4):
                    tt_in_seq = (gi % 4) * 4 + ts
                    for eb in range(4):
                        k = gi * 16 + ts * 4 + eb
                        s_ve.wait(g, k + 1)
                        for hh in range(4):
                            h = eb * 4 + hh
                            s_stv.inc(g.dma_start(
                                out=self.v_scr[s, h, :, tt_in_seq, :],
                                in_=vev[k % 2][:, hh * 128:(hh + 1) * 128]), 16)
            s_stk.wait(g)
            s_stv.wait(g)
            s_gts.wait(g)

        self.blk.sync(sy)
        self.blk.tensor(pe)
        self.blk.scalar(ac)
        self.blk.vector(ve)
        self.blk.gpsimd(gp)
        self.emit_barrier()
        stk.close()

    @staticmethod
    def p4_batches(qg):
        out = []
        for pqs in range(4):
            n = qg * 4 + pqs + 1
            kt0 = 0
            while n > 0:
                nb = min(8, n)
                out.append((pqs, kt0, nb))
                kt0 += nb
                n -= nb
        return out

    def ph_p34(self):
        stk = ExitStack()
        kh = [self.sb(stk, f"p4_k{i}", [128, SEQ], BF16) for i in range(2)]
        vh = [self.sb(stk, f"p4_v{i}", [128, SEQ], BF16) for i in range(2)]
        qh = [self.sb(stk, f"p4_q{i}", [128, 512], BF16) for i in range(2)]
        pt = self.sb(stk, "p4_pt", [128, 16 * 512], BF16)
        pb = [self.sb(stk, f"p4_pb{i}", [128, SEQ], BF16) for i in range(2)]
        cols = self.sb(stk, "p4_cols", [128, 16], F32)
        rrow = self.sb(stk, "p4_rrow", [1, 512], F32)
        rbc = self.sb(stk, "p4_rbc", [128, 512], F32)
        aoe = [self.sb(stk, f"p4_ao{i}", [128, 512], BF16) for i in range(2)]
        xnr = [self.sb(stk, f"p3_xn{dc}", [128, T], BF16) for dc in range(16)]
        gT = [self.sb(stk, f"p3_g{i}", [128, 2048], BF16) for i in range(2)]
        uT = [self.sb(stk, f"p3_u{i}", [128, 2048], BF16) for i in range(2)]
        sil = [self.sb(stk, f"p3_sil{i}", [128, 512], F32) for i in range(2)]
        hbf = [self.sb(stk, f"p3_h{i}", [128, 512], BF16) for i in range(2)]
        dstg = [self.sb(stk, "p34_ds0", [128, 2048], BF16)] * 2
        sps = self.psum(stk, "p4_sps", [128, SEQ], F32)
        tps = self.psum(stk, "p4_tps", [128, 2048], BF16)
        aor = self.psum(stk, "p4_aor", [128, 512], F32)
        gup = self.psum(stk, "p3_gup", [128, 512], F32)
        gps = gup
        ups = gup

        s_kv = self.ctr("p4_kv")
        s_q = self.ctr("p4_q")
        s_sc = self.ctr("p4_sc")
        s_mx = self.ctr("p4_mx")
        s_ex = self.ctr("p4_ex")
        s_rc = self.ctr("p4_rc")
        s_tp = self.ctr("p4_tp")
        s_te = self.ctr("p4_te")
        s_rr = self.ctr("p4_rr")
        s_re = self.ctr("p4_re")
        s_rb = self.ctr("p4_rb")
        s_rbe = self.ctr("p4_rbe")
        s_pv = self.ctr("p4_pv")
        s_ae = self.ctr("p4_ae")
        s_sta = self.ctr("p4_sta")
        s_xnr = self.ctr("p3_xnr")
        s_w3 = self.ctr("p3_w")
        s_gm = self.ctr("p3_gm")
        s_um = self.ctr("p3_um")
        s_sil = self.ctr("p3_sil")
        s_hd = self.ctr("p3_hd")
        s_st3 = self.ctr("p3_st")
        s_dt = self.ctr("p34_dt")
        s_dts = self.ctr("p34_dts")

        SH = [(s, h) for s in range(NSEQ) for h in range(H)]

        att_ev = []
        for j in range(32):
            for qg in range(4):
                jq = j * 4 + qg
                batches = self.p4_batches(qg)
                ntr = {p: sum(nb for q_, _, nb in batches if q_ == p) for p in range(4)}
                if jq == 0:
                    att_ev.append((("sc", 0, 0), 800))
                for qs in range(1, 4):
                    c = qg * 512 + (qs + 1) * 128 + 134 + ntr[qs - 1] * 140
                    att_ev.append((("sc", jq, qs), c))
                att_ev.append((("t1", jq), ntr[3] * 140 + 100))
                att_ev.append((("t2", jq), 700))
                if jq + 1 < 128:
                    att_ev.append((("sc", jq + 1, 0), (jq % 4 == 3) * 0 + ((jq + 1) % 4) * 512 + 262))
                att_ev.append((("t3", jq), 2100))
                att_ev.append((("t4", jq), (4 * qg + 4) * 512 + 200))
        p3_ev = []
        for ic in range(NIC):
            for tg in range(NTG):
                p3_ev.append((("pg", ic, tg), 16 * 512 + 128))
                p3_ev.append((("pu", ic, tg), 16 * 512 + 128))
        tot_att = sum(c for _, c in att_ev)
        tot_p3 = sum(c for _, c in p3_ev)
        ratio = tot_p3 / tot_att
        sched = []
        acc_a = acc_p = 0.0
        pi = 0
        for ev, c in att_ev:
            sched.append(ev)
            acc_a += c
            npump = 0
            while pi < len(p3_ev) and acc_p < acc_a * ratio and npump < 2:
                sched.append(p3_ev[pi][0])
                acc_p += p3_ev[pi][1]
                pi += 1
                npump += 1
        while pi < len(p3_ev):
            sched.append(p3_ev[pi][0])
            pi += 1

        GB = {}
        gb_running = 0
        for jq in range(128):
            batches = self.p4_batches(jq % 4)
            GB[jq] = (gb_running, batches)
            gb_running += len(batches)

        pieces = [(self.dw16, self.dw16T, k) for k in range(NIC)] + \
                 [(self.wo16, self.wo16T, k) for k in range(16)]
        pidx = [0]

        def emit_piece(sy):
            if pidx[0] >= len(pieces):
                return
            src, dstT, k = pieces[pidx[0]]
            i = pidx[0]
            if i == 0:
                self.pc["dw"].wait(sy, (D // 256) * 16)
            if src is self.wo16 and k == 0:
                self.pc["wo"].wait(sy, (D // 512) * 16)
            if i >= 1:
                s_dts.wait(sy, i * 16)
            s_dt.inc(sy.dma_start_transpose(
                out=dstg[0][:], in_=src[:, k * 128:(k + 1) * 128]), 16)
            s_dt.wait(sy, (i + 1) * 16)
            s_dts.inc(sy.dma_start(
                out=dstT[k * 128:(k + 1) * 128, :], in_=dstg[0][:]), 16)
            pidx[0] += 1

        def sy(sy):
            s_kv.inc(sy.dma_start(out=kh[0][:],
                                  in_=self.k_scr[SH[0][1]][:, 0:SEQ]), 16)
            s_kv.inc(sy.dma_start(
                out=vh[0][:].rearrange("k (a c) -> k a c", a=16),
                in_=self.v_scr[0, SH[0][1]]), 16)
            s_q.inc(sy.dma_start(out=qh[0][:], in_=self.q_scr[SH[0][1]][:, 0:512]), 16)
            for dc in range(16):
                s_xnr.inc(sy.dma_start(
                    out=xnr[dc][:].rearrange("p (g t) -> p g t", g=NTG),
                    in_=self.xn_scr[:, :, dc, :].rearrange("g p t -> p g t")), 16)
            def load_w(ic):
                for buf, w16T in ((gT, self.g16T), (uT, self.u16T)):
                    s_w3.inc(sy.dma_start(
                        out=buf[ic % 2][:].rearrange("p (dc i) -> p dc i", i=128),
                        in_=w16T.rearrange("(dc p) i -> p dc i", p=128)[
                            :, :, ic * 128:(ic + 1) * 128]), 16)

            for ic0 in (0, 1):
                load_w(ic0)
            for ev in sched:
                if ev[0] == "sc":
                    _, jq, qs = ev
                    if qs != 0:
                        continue
                    j, qg = jq // 4, jq % 4
                    if qg == 0 and j + 1 < 32:
                        jn = j + 1
                        s, h = SH[jn]
                        if jn >= 2:
                            s_pv.wait(sy, (jn - 1) * 4)
                        s_kv.inc(sy.dma_start(
                            out=kh[jn % 2][:],
                            in_=self.k_scr[h][:, s * SEQ:(s + 1) * SEQ]), 16)
                        s_kv.inc(sy.dma_start(
                            out=vh[jn % 2][:].rearrange("k (a c) -> k a c", a=16),
                            in_=self.v_scr[s, h]), 16)
                    if jq + 1 < 128:
                        jqn = jq + 1
                        jn2, qgn = jqn // 4, jqn % 4
                        s2, h2 = SH[jn2]
                        if jqn >= 2:
                            s_pv.wait(sy, jqn - 1)
                        s_q.inc(sy.dma_start(
                            out=qh[jqn % 2][:],
                            in_=self.q_scr[h2][:, s2 * SEQ + qgn * 512:
                                               s2 * SEQ + (qgn + 1) * 512]), 16)
                    if jq >= 2 and jq % 2 == 0:
                        emit_piece(sy)
                elif ev[0] == "pu":
                    _, ic, tg = ev
                    if tg == NTG - 1 and ic + 2 < NIC:
                        s_um.wait(sy, (ic + 1) * 8)
                        load_w(ic + 2)


        def sy_tail(sy):
            while pidx[0] < len(pieces):
                emit_piece(sy)
            s_dts.wait(sy)

        def pe(t):
            first_p3 = True
            gb = {}
            for ev in sched:
                if ev[0] == "sc":
                    _, jq, qs = ev
                    j, qg = jq // 4, jq % 4
                    iq = jq * 4 + qs
                    qt = qg * 4 + qs
                    if qs == 0:
                        t.wait_ge(s_kv.sem, (j + 1) * 32)
                        t.wait_ge(s_q.sem, (jq + 1) * 16)
                    soff = (qs % 2) * (qg + 1) * 512 if qg <= 1 else 0
                    if qg <= 1 and qs >= 1:
                        s_ex.wait(t, iq - 1)
                    else:
                        s_ex.wait(t, iq)
                    lhs = qh[jq % 2][:, qs * 128:(qs + 1) * 128]
                    for kc in range(qg):
                        ins = t.matmul(sps[:, soff + kc * 512:soff + (kc + 1) * 512],
                                       lhs,
                                       kh[j % 2][:, kc * 512:(kc + 1) * 512],
                                       start=True, stop=True)
                        if kc == qg - 1:
                            s_sc.inc(ins)
                    bank = sps[:, soff + qg * 512:soff + (qg + 1) * 512]
                    for ks in range(qs + 1):
                        kt = qg * 4 + ks
                        ins = t.matmul(bank[:, ks * 128:(ks + 1) * 128], lhs,
                                       kh[j % 2][:, kt * 128:(kt + 1) * 128],
                                       start=(ks == 0), stop=False)
                        if qg == 0 and ks == qs:
                            s_sc.inc(ins)
                    s_sc.inc(t.matmul(bank[:, qs * 128:(qs + 1) * 128],
                                      self.ident_bf[:], self.maskU[:],
                                      start=False, stop=True))
                    if qs >= 1:
                        s_ex.wait(t, iq)
                        gb0, batches = GB[jq]
                        pos = gb.get(jq, 0)
                        while pos < len(batches) and batches[pos][0] == qs - 1:
                            pqs, kt0, nb = batches[pos]
                            half = ((gb0 + pos) % 2) * 1024
                            if gb0 + pos >= 1:
                                s_te.wait(t, gb0 + pos - 1)
                            for u in range(nb):
                                ins = t.transpose(
                                    tps[:, half + u * 128:half + (u + 1) * 128],
                                    pb[pqs % 2][:, (kt0 + u) * 128:(kt0 + u + 1) * 128],
                                    self.ident_bf[:])
                            s_tp.inc(ins)
                            pos += 1
                        gb[jq] = pos
                elif ev[0] == "t1":
                    jq = ev[1]
                    s_ex.wait(t, jq * 4 + 4)
                    gb0, batches = GB[jq]
                    pos = gb.get(jq, 0)
                    while pos < len(batches):
                        pqs, kt0, nb = batches[pos]
                        half = ((gb0 + pos) % 2) * 1024
                        if gb0 + pos >= 1:
                            s_te.wait(t, gb0 + pos - 1)
                        for u in range(nb):
                            ins = t.transpose(
                                tps[:, half + u * 128:half + (u + 1) * 128],
                                pb[pqs % 2][:, (kt0 + u) * 128:(kt0 + u + 1) * 128],
                                self.ident_bf[:])
                        s_tp.inc(ins)
                        pos += 1
                    gb[jq] = pos
                elif ev[0] == "t2":
                    jq = ev[1]
                    s_ae.wait(t, jq)
                    s_rc.wait(t, 4 * jq + 4)
                    par = 8 + (jq % 2) * 4
                    for qs in range(4):
                        ins = t.transpose(aor[0:1, qs * 128:(qs + 1) * 128],
                                          cols[:, par + qs:par + qs + 1],
                                          self.ident_f[:])
                    s_rr.inc(ins)
                elif ev[0] == "t3":
                    jq = ev[1]
                    s_re.wait(t, jq + 1)
                    s_rb.inc(t.matmul(aor[:], self.ones_col[:], rrow[0:1, :],
                                      start=True, stop=True))
                elif ev[0] == "t4":
                    jq = ev[1]
                    j, qg = jq // 4, jq % 4
                    gb0, batches = GB[jq]
                    s_rbe.wait(t, jq + 1)
                    s_te.wait(t, gb0 + len(batches))
                    for kt in range(4 * qg + 4):
                        ki = kt - 4 * qg
                        lo = max(0, ki) * 128
                        ins = t.matmul(aor[:, lo:512],
                                       vh[j % 2][:, kt * 128:(kt + 1) * 128],
                                       pt[:, kt * 512 + lo:(kt + 1) * 512],
                                       start=(kt == 0), stop=(kt == 4 * qg + 3))
                    s_pv.inc(ins)
                elif ev[0] == "pg":
                    _, ic, tg = ev
                    k = ic * NTG + tg
                    if first_p3:
                        s_xnr.wait(t)
                        first_p3 = False
                    if tg == 0:
                        s_w3.wait(t, (ic + 1) * 2 * 16)
                    s_hd.wait(t, k)
                    for dc in range(16):
                        ins = t.matmul(gps[:], gT[ic % 2][:, dc * 128:(dc + 1) * 128],
                                       xnr[dc][:, tg * 512:(tg + 1) * 512],
                                       start=(dc == 0), stop=(dc == 15))
                    s_gm.inc(ins)
                else:
                    _, ic, tg = ev
                    k = ic * NTG + tg
                    s_sil.wait(t, k + 1)
                    for dc in range(16):
                        ins = t.matmul(ups[:], uT[ic % 2][:, dc * 128:(dc + 1) * 128],
                                       xnr[dc][:, tg * 512:(tg + 1) * 512],
                                       start=(dc == 0), stop=(dc == 15))
                    s_um.inc(ins)

        def ac(a):
            for ev in sched:
                if ev[0] == "sc":
                    _, jq, qs = ev
                    qg = jq % 4
                    iq = jq * 4 + qs
                    L = (qg * 4 + qs + 1) * 128
                    soff = (qs % 2) * (qg + 1) * 512 if qg <= 1 else 0
                    s_mx.wait(a, iq + 1)
                    s_ex.inc(a.activation(pb[qs % 2][:, 0:L], sps[:, soff:soff + L],
                                          AF.Exp, bias=cols[:, qs % 2:qs % 2 + 1],
                                          scale=1.0,
                                          accum_out=cols[:, 4 + qs:5 + qs]))
                elif ev[0] == "pg":
                    _, ic, tg = ev
                    k = ic * NTG + tg
                    s_gm.wait(a, k + 1)
                    if k >= 2:
                        s_hd.wait(a, k - 1)
                    s_sil.inc(a.activation(sil[k % 2][:], gps[:], AF.Silu,
                                           bias=0.0, scale=self.gs_cols[:, ic:ic + 1]))

        def ve(v):
            gb = {}
            for ev in sched:
                if ev[0] == "sc":
                    _, jq, qs = ev
                    qg = jq % 4
                    iq = jq * 4 + qs
                    L = (qg * 4 + qs + 1) * 128
                    par = 8 + (jq % 2) * 4
                    soff = (qs % 2) * (qg + 1) * 512 if qg <= 1 else 0
                    if qg >= 1:
                        s_sc.wait(v, 2 * iq + 1)
                        v.reduce_max(cols[:, 2:3], sps[:, soff:soff + qg * 512],
                                     axis=AX.X, negate=True)
                        s_sc.wait(v, 2 * iq + 2)
                        self.vs(v, v.reduce_max(cols[:, 3:4],
                                                sps[:, soff + qg * 512:soff + L],
                                                axis=AX.X, negate=True))
                        s_mx.inc(v.tensor_tensor(cols[:, qs % 2:qs % 2 + 1],
                                                 cols[:, 2:3],
                                                 cols[:, 3:4], ALU.min))
                    else:
                        s_sc.wait(v, 2 * iq + 2)
                        s_mx.inc(v.reduce_max(cols[:, qs % 2:qs % 2 + 1],
                                              sps[:, soff:soff + L],
                                              axis=AX.X, negate=True))
                    s_ex.wait(v, iq + 1)
                    s_rc.inc(v.reciprocal(cols[:, par + qs:par + qs + 1],
                                          cols[:, 4 + qs:5 + qs]))
                    if qs >= 1:
                        gb0, batches = GB[jq]
                        pos = gb.get(jq, 0)
                        while pos < len(batches) and batches[pos][0] == qs - 1:
                            pqs, kt0, nb = batches[pos]
                            s_tp.wait(v, gb0 + pos + 1)
                            half = ((gb0 + pos) % 2) * 1024
                            dst = pt[:].rearrange("p (kt q) -> p kt q", kt=16)[
                                :, kt0:kt0 + nb, pqs * 128:(pqs + 1) * 128]
                            src = tps[:, half:half + nb * 128].rearrange(
                                "p (a b) -> p a b", b=128)
                            s_te.inc(v.tensor_copy(dst, src))
                            pos += 1
                        gb[jq] = pos
                elif ev[0] == "t1":
                    jq = ev[1]
                    gb0, batches = GB[jq]
                    pos = gb.get(jq, 0)
                    while pos < len(batches):
                        pqs, kt0, nb = batches[pos]
                        s_tp.wait(v, gb0 + pos + 1)
                        half = ((gb0 + pos) % 2) * 1024
                        dst = pt[:].rearrange("p (kt q) -> p kt q", kt=16)[
                            :, kt0:kt0 + nb, pqs * 128:(pqs + 1) * 128]
                        src = tps[:, half:half + nb * 128].rearrange(
                            "p (a b) -> p a b", b=128)
                        s_te.inc(v.tensor_copy(dst, src))
                        pos += 1
                    gb[jq] = pos
                elif ev[0] == "t2":
                    jq = ev[1]
                    s_rr.wait(v, jq + 1)
                    s_re.inc(v.tensor_copy(rrow[0:1, :], aor[0:1, 0:512]))
                elif ev[0] == "t3":
                    jq = ev[1]
                    s_rb.wait(v, jq + 1)
                    s_rbe.inc(v.tensor_copy(rbc[:], aor[:]))
                    s_rbe.wait(v)
                elif ev[0] == "t4":
                    jq = ev[1]
                    s_pv.wait(v, jq + 1)
                    if jq >= 2:
                        s_sta.wait(v, (jq - 1) * 16)
                    s_ae.inc(v.scalar_tensor_tensor(aoe[jq % 2][:], aor[:], 1.0,
                                                    rbc[:], ALU.bypass, ALU.mult))
                elif ev[0] == "pu":
                    _, ic, tg = ev
                    k = ic * NTG + tg
                    s_um.wait(v, k + 1)
                    s_sil.wait(v, k + 1)
                    if k >= 2:
                        s_st3.wait(v, (k - 1) * 16)
                    s_hd.inc(v.scalar_tensor_tensor(
                        hbf[k % 2][:], ups[:], self.us_cols[:, ic:ic + 1],
                        sil[k % 2][:], ALU.mult, ALU.mult))

        def gp(g):
            for src_t, dst, nrows, width, ctr, step in (
                    (self.down_w, self.dw16, D, INTER, self.pc["dw"], 256),
                    (self.w_out, self.wo16, D, D, self.pc["wo"], 512)):
                for r0 in range(0, nrows, step):
                    r1 = min(r0 + step, nrows)
                    ctr.inc(g.dma_start(out=dst[r0:r1, :], in_=src_t[r0:r1, :]), 16)
            for ev in sched:
                if ev[0] == "t4":
                    jq = ev[1]
                    if jq >= 1:
                        s_ae.wait(g, jq)
                        s_sta.inc(g.dma_start(out=self._ao_dst(jq - 1, SH),
                                              in_=aoe[(jq - 1) % 2][:]), 16)
                elif ev[0] == "pu":
                    _, ic, tg = ev
                    k = ic * NTG + tg
                    s_hd.wait(g, k + 1)
                    s_st3.inc(g.dma_start(
                        out=self.hdn_scr[ic][:, tg * 512:(tg + 1) * 512],
                        in_=hbf[k % 2][:]), 16)
            s_ae.wait(g, 128)
            s_sta.inc(g.dma_start(out=self._ao_dst(127, SH), in_=aoe[127 % 2][:]), 16)
            s_sta.wait(g)
            s_st3.wait(g)

        self.blk.sync(sy)
        self.blk.sync(sy_tail)
        self.blk.tensor(pe)
        self.blk.scalar(ac)
        self.blk.vector(ve)
        self.blk.gpsimd(gp)
        self.emit_barrier()
        stk.close()

    def _ao_dst(self, jq, SH):
        j, qg = jq // 4, jq % 4
        s, h = SH[j]
        return self.ao_scr[h][:, s * SEQ + qg * 512:s * SEQ + (qg + 1) * 512]

    def ph_p5(self):
        stk = ExitStack()
        dwT = [self.sb(stk, f"p5_dw{ic}", [128, 1024], BF16) for ic in range(NIC)]
        slab = [self.sb(stk, f"p5_sl{i}", [128, NIC * 128], BF16) for i in range(2)]
        fev = [self.sb(stk, f"p5_f{i}", [128, 512], BF16) for i in range(2)]
        dps = [self.psum(stk, f"p5_dp{i}", [128, 512], F32) for i in range(2)]

        s_w = self.ctr("p5_w")
        s_sl = self.ctr("p5_sl")
        s_dm = self.ctr("p5_dm")
        s_de = self.ctr("p5_de")
        s_st = self.ctr("p5_st")
        s_hf = self.ctr("p5_hf")

        def sy(sy):
            for half in range(2):
                if half == 1:
                    s_dm.wait(sy, 64)
                for ic in range(NIC):
                    s_w.inc(sy.dma_start(
                        out=dwT[ic][:],
                        in_=self.dw16T[ic * 128:(ic + 1) * 128,
                                       half * 1024:(half + 1) * 1024]), 16)
                for tt in range(NTT):
                    i = half * NTT + tt
                    if i >= 2:
                        s_dm.wait(sy, (i - 1) * 2)
                    for b in range(4):
                        nic = 11
                        s_sl.inc(sy.dma_start(
                            out=slab[i % 2][:, b * nic * 128:(b + 1) * nic * 128].rearrange(
                                "p (a t) -> p a t", a=nic),
                            in_=self.hdn_scr[b * nic:(b + 1) * nic, :,
                                             tt * 128:(tt + 1) * 128].rearrange(
                                "a p t -> p a t")), 16)

        def pe(t):
            for half in range(2):
                s_w.wait(t, (half + 1) * NIC * 16)
                for tt in range(NTT):
                    i = half * NTT + tt
                    s_sl.wait(t, (i + 1) * 64)
                    for hb in range(2):
                        k = i * 2 + hb
                        if k >= 2:
                            s_de.wait(t, k - 1)
                        for ic in range(NIC):
                            ins = t.matmul(dps[k % 2][:],
                                           slab[i % 2][:, ic * 128:(ic + 1) * 128],
                                           dwT[ic][:, hb * 512:(hb + 1) * 512],
                                           start=(ic == 0), stop=(ic == NIC - 1))
                        s_dm.inc(ins)

        def ve(v):
            for half in range(2):
                for tt in range(NTT):
                    for hb in range(2):
                        k = (half * NTT + tt) * 2 + hb
                        s_dm.wait(v, k + 1)
                        if k >= 2:
                            s_st.wait(v, (k - 1) * 16)
                        s_de.inc(v.tensor_copy(fev[k % 2][:], dps[k % 2][:]))

        def gp(g):
            for half in range(2):
                for tt in range(NTT):
                    for hb in range(2):
                        k = (half * NTT + tt) * 2 + hb
                        s_de.wait(g, k + 1)
                        s_st.inc(g.dma_start(
                            out=self.ffn_scr[tt][:, half * 1024 + hb * 512:
                                                half * 1024 + (hb + 1) * 512],
                            in_=fev[k % 2][:]), 16)
            s_st.wait(g)

        self.blk.sync(sy)
        self.blk.tensor(pe)
        self.blk.vector(ve)
        self.blk.gpsimd(gp)
        self.blk.scalar(lambda a: None)
        self.emit_barrier()
        stk.close()

    def ph_p6(self):
        stk = ExitStack()
        woT = [self.sb(stk, f"p6_wo{dc}", [128, D], BF16) for dc in range(16)]
        aog = [[self.sb(stk, f"p6_a{i}_{hh}", [128, 512], BF16) for hh in range(16)]
               for i in range(2)]
        xb = [self.sb(stk, f"p6_x{i}", [128, D], F32) for i in range(2)]
        fb = [self.sb(stk, f"p6_f{i}", [128, D], BF16) for i in range(2)]
        yb = [self.sb(stk, f"p6_y{i}", [128, D], F32) for i in range(2)]
        acc1 = [self.sb(stk, f"p6_c1{i}", [128, D], F32) for i in range(2)]
        acc2 = self.sb(stk, "p6_c2", [128, D], F32)
        borow = self.sb(stk, "p6_bor", [1, D], F32)
        dsrow = self.sb(stk, "p6_dsr", [1, D], F32)
        bo_bc = self.sb(stk, "p6_bobc", [128, D], F32)
        ds_bc = self.sb(stk, "p6_dsbc", [128, D], F32)
        prps = [self.psum(stk, f"p6_pp{i}", [128, D], F32) for i in range(2)]

        s_r = self.ctr("p6_r")
        s_bm = self.ctr("p6_bm")
        s_be = self.ctr("p6_be")
        s_wo = self.ctr("p6_wo")
        s_ao = self.ctr("p6_ao")
        s_x = self.ctr("p6_x")
        s_f = self.ctr("p6_f")
        s_pm = self.ctr("p6_pm")
        s_ye = self.ctr("p6_ye")
        s_st = self.ctr("p6_st")

        def sy(sy):
            s_r.inc(sy.dma_start(out=borow[:], in_=self.b_out[None, :]), 16)
            s_r.inc(sy.dma_start(out=dsrow[:], in_=self.down_s[None, :]), 16)
            for dc in range(16):
                s_wo.inc(sy.dma_start(
                    out=woT[dc][:],
                    in_=self.wo16T[dc * 128:(dc + 1) * 128, :]), 16)
            for g in range(NTG):
                if g >= 2:
                    s_pm.wait(sy, (g - 1) * 4)
                for hh in range(16):
                    s_ao.inc(sy.dma_start(
                        out=aog[g % 2][hh][:],
                        in_=self.ao_scr[hh][:, g * 512:(g + 1) * 512]), 16)
                for ts in range(4):
                    tt = g * 4 + ts
                    s = tt // 16
                    r = (tt % 16) * 128
                    if tt >= 2:
                        s_ye.wait(sy, tt - 1)
                    s_x.inc(sy.dma_start(out=xb[tt % 2][:],
                                         in_=self.x[s, r:r + 128, :]), 16)
                    s_f.inc(sy.dma_start(out=fb[tt % 2][:], in_=self.ffn_scr[tt]), 16)

        def pe(t):
            s_r.wait(t)
            for b in range(8):
                if b >= 1:
                    s_be.wait(t, b)
                row = borow if b < 4 else dsrow
                s_bm.inc(t.matmul(prps[0][:, 0:512], self.ones_col[:],
                                  row[0:1, (b % 4) * 512:(b % 4 + 1) * 512],
                                  start=True, stop=True))
            s_wo.wait(t)
            s_be.wait(t, 8)
            for tt in range(NTT):
                g, ts = tt // 4, tt % 4
                s_ao.wait(t, (g + 1) * 256)
                if tt >= 2:
                    s_ye.wait(t, tt - 1)
                for eb in range(4):
                    for dc in range(16):
                        ins = t.matmul(prps[tt % 2][:, eb * 512:(eb + 1) * 512],
                                       aog[g % 2][dc][:, ts * 128:(ts + 1) * 128],
                                       woT[dc][:, eb * 512:(eb + 1) * 512],
                                       start=(dc == 0), stop=(dc == 15))
                s_pm.inc(ins)

        def ve(v):
            for b in range(8):
                s_bm.wait(v, b + 1)
                dst = bo_bc if b < 4 else ds_bc
                s_be.inc(v.tensor_copy(dst[:, (b % 4) * 512:(b % 4 + 1) * 512],
                                       prps[0][:, 0:512]))
            for tt in range(NTT):
                s_f.wait(v, (tt + 1) * 16)
                s_x.wait(v, (tt + 1) * 16)
                v.scalar_tensor_tensor(acc1[tt % 2][:], fb[tt % 2][:], 1.0,
                                       ds_bc[:], ALU.bypass, ALU.mult)
                self.vs(v, v.scalar_tensor_tensor(acc2[:], xb[tt % 2][:], 1.0,
                                                  bo_bc[:], ALU.bypass, ALU.add))
                self.vs(v, v.tensor_tensor(acc1[tt % 2][:], acc1[tt % 2][:],
                                           acc2[:], ALU.add))
                s_pm.wait(v, tt + 1)
                if tt >= 2:
                    s_st.wait(v, (tt - 1) * 16)
                s_ye.inc(v.scalar_tensor_tensor(yb[tt % 2][:], prps[tt % 2][:], 1.0,
                                                acc1[tt % 2][:], ALU.bypass, ALU.add))

        def gp(g):
            for tt in range(NTT):
                s = tt // 16
                r = (tt % 16) * 128
                s_ye.wait(g, tt + 1)
                s_st.inc(g.dma_start(out=self.y[s, r:r + 128, :], in_=yb[tt % 2][:]), 16)
            s_st.wait(g)

        self.blk.sync(sy)
        self.blk.tensor(pe)
        self.blk.vector(ve)
        self.blk.gpsimd(gp)
        self.blk.scalar(lambda a: None)
        stk.close()


def build_nc(dbg=()):
    return B(dbg=dbg).build()


_NC_CACHE = {}


def kernel(**inputs):
    key = "main"
    if key not in _NC_CACHE:
        _NC_CACHE[key] = build_nc()
    nc = _NC_CACHE[key]
    x = np.ascontiguousarray(inputs["L0_attn_x"])
    base = {
        "w_qkv": np.ascontiguousarray(inputs["L0_attn_w_qkv"]),
        "b_qkv": np.ascontiguousarray(inputs["L0_attn_b_qkv"]),
        "w_out": np.ascontiguousarray(inputs["L0_attn_w_out"]),
        "b_out": np.ascontiguousarray(inputs["L0_attn_b_out"]),
        "gate_w": np.ascontiguousarray(inputs["L0_ffn_gate_w"]),
        "up_w": np.ascontiguousarray(inputs["L0_ffn_up_w"]),
        "down_w": np.ascontiguousarray(inputs["L0_ffn_down_w"]),
        "gate_s": np.ascontiguousarray(inputs["L0_ffn_gate_s"]),
        "up_s": np.ascontiguousarray(inputs["L0_ffn_up_s"]),
        "down_s": np.ascontiguousarray(inputs["L0_ffn_down_s"]),
    }
    in_maps = [dict(base, x=np.ascontiguousarray(x[2 * i:2 * i + 2]))
               for i in range(8)]
    res = run_bass_kernel_spmd(nc, in_maps, core_ids=list(range(8)))
    return np.concatenate([r["y"] for r in res.results], axis=0)
